# version 31
# baseline (speedup 1.0000x reference)
import os
import numpy as np
import ml_dtypes
from contextlib import ExitStack

import concourse.bass as bass
import concourse.mybir as mybir
from concourse.bass_utils import run_bass_kernel_spmd

F32 = mybir.dt.float32
BF16 = mybir.dt.bfloat16
FP16 = mybir.dt.float16
AF = mybir.ActivationFunctionType
ALU = mybir.AluOpType
AX = mybir.AxisListType.X
BF = ml_dtypes.bfloat16

B, S, D = 4, 8192, 1024
NCORES = 8
TPC = S // NCORES
NCH = TPC // 128
NG = TPC // 512
EPS_ATTN = 1e-6
EPS_LN = 1e-5


class Prog:

    ENGINES = ("tensor", "vector", "scalar", "gpsimd", "sync")

    def __init__(self, nc, sems):
        self.nc = nc
        self.sems = sems
        self.streams = {e: [] for e in self.ENGINES}
        self.cnt = {}
        self.last_wait = {e: {} for e in self.ENGINES}
        self.pending = {e: [] for e in self.ENGINES}
        self.auto_self = {}

    def emit(self, engine, fn, waits=(), inc=None, inc_n=1):
        auto = self.auto_self.get(engine)
        waits = list(waits)
        if auto is not None:
            waits.append((auto, self.val(auto)))
            if inc is None:
                inc = auto
        w = []
        for item in list(self.pending[engine]) + list(waits):
            if item is None:
                continue
            sem, val = item
            if val is None or val <= 0:
                continue
            lw = self.last_wait[engine].get(sem.name, 0)
            if val > lw:
                w.append((sem, val))
                self.last_wait[engine][sem.name] = val
        self.pending[engine] = []
        self.streams[engine].append((tuple(w), fn, inc, inc_n))
        if inc is not None:
            self.cnt[inc.name] = self.cnt.get(inc.name, 0) + inc_n
            return self.cnt[inc.name]
        return None

    def val(self, sem):
        return self.cnt.get(sem.name, 0)

    def fence(self):
        snap = [(s, self.val(s)) for s in self.sems]
        for e in self.ENGINES:
            self.pending[e] = list(snap)

    def finalize(self, block):
        for engine in self.ENGINES:
            stream = self.streams[engine]

            def make(engine_stream):
                def body(eng):
                    for waits, fn, inc, inc_n in engine_stream:
                        for semobj, val in waits:
                            eng.wait_ge(semobj, val)
                        instr = fn(eng)
                        if inc is not None:
                            instr.then_inc(inc, inc_n)
                return body
            getattr(block, engine)(make(stream))


def build_program():
    nc = bass.Bass("TRN2", target_bir_lowering=False, debug=False)

    xs_d = nc.dram_tensor("xs", [B, TPC, D], F32, kind="ExternalInput").ap()
    wq_d = nc.dram_tensor("wq", [D, D], F32, kind="ExternalInput").ap()
    wk_d = nc.dram_tensor("wk", [D, D], F32, kind="ExternalInput").ap()
    wv_d = nc.dram_tensor("wv", [D, D], F32, kind="ExternalInput").ap()
    wo_d = nc.dram_tensor("wo", [D, D], F32, kind="ExternalInput").ap()
    wr_d = nc.dram_tensor("wr", [D, D], F32, kind="ExternalInput").ap()
    wi_d = nc.dram_tensor("wi", [D, D], F32, kind="ExternalInput").ap()
    bias_d = nc.dram_tensor("bias", [1, D], F32, kind="ExternalInput").ap()
    cfc_d = nc.dram_tensor("cfc", [D, D], F32, kind="ExternalInput").ap()
    cfs_d = nc.dram_tensor("cfs", [D, D], F32, kind="ExternalInput").ap()
    cs1_d = nc.dram_tensor("cs1", [2, D, 128], F32, kind="ExternalInput").ap()
    maskc_d = nc.dram_tensor("maskc", [128, 128], BF16, kind="ExternalInput").ap()
    idf_d = nc.dram_tensor("idf", [128, 128], F32, kind="ExternalInput").ap()
    ones16_d = nc.dram_tensor("ones16", [1, 128], FP16, kind="ExternalInput").ap()
    id16_d = nc.dram_tensor("id16", [128, 128], FP16, kind="ExternalInput").ap()
    idb_d = nc.dram_tensor("idb", [128, 128], BF16, kind="ExternalInput").ap()
    mask8_d = nc.dram_tensor("mask8", [128, 8], F32, kind="ExternalInput").ap()

    y_d = nc.dram_tensor("y", [B, TPC, D], F32, kind="ExternalOutput").ap()

    scr_xb = nc.dram_tensor("scr_xb", [TPC, D], BF16, kind="Internal")
    scr_k = [nc.dram_tensor(f"scr_k{i}", [D, 128], BF16, kind="Internal") for i in range(3)]
    scr_o = [nc.dram_tensor(f"scr_o{i}", [128, D], BF16, kind="Internal") for i in range(2)]
    scr_f = [nc.dram_tensor(f"scr_f{i}", [128, D], FP16, kind="Internal") for i in range(2)]
    sg_in = nc.dram_tensor("sg_in", [128, 520], F32, kind="Internal")
    sg_out = nc.dram_tensor("sg_out", [8 * 128, 520], F32, kind="Internal")
    mg_in = nc.dram_tensor("mg_in", [128, D], FP16, kind="Internal")
    mg_out = nc.dram_tensor("mg_out", [8 * 128, D], FP16, kind="Internal")

    es = ExitStack()
    dSs = [es.enter_context(nc.semaphore(f"dS{i}")) for i in range(8)]
    dAs = [es.enter_context(nc.semaphore(f"dA{i}")) for i in range(4)]
    dGs = [es.enter_context(nc.semaphore(f"dG{i}")) for i in range(4)]
    ccS = es.enter_context(nc.semaphore("ccS"))
    cP = es.enter_context(nc.semaphore("cP"))
    cV = es.enter_context(nc.semaphore("cV"))
    cA = es.enter_context(nc.semaphore("cA"))
    cG = es.enter_context(nc.semaphore("cG"))
    p = Prog(nc, tuple(dSs + dAs + dGs) + (ccS, cP, cV, cA, cG))
    p.auto_self = {"vector": cV, "scalar": cA, "gpsimd": cG}

    qrot = {"sync": [dSs, 0], "scalar": [dAs, 0], "gpsimd": [dGs, 0]}

    def dma(qe, out, in_, sem_unused=None, waits=(), **kw):
        fam = qrot[qe]
        s = fam[0][fam[1]]
        fam[1] = (fam[1] + 1) % len(fam[0])
        throttle = (s, p.val(s))
        tok = p.emit(qe, lambda e, out=out, in_=in_, kw=kw: e.dma_start(out=out, in_=in_, **kw),
                     waits=list(waits) + [throttle], inc=s, inc_n=16)
        return (s, tok)

    mes = ExitStack()
    w32 = mes.enter_context(nc.sbuf_tensor("w32", [128, 16384], F32))
    cs1 = mes.enter_context(nc.sbuf_tensor("cs1s", [128, 2048], F32))
    cful = mes.enter_context(nc.sbuf_tensor("cful", [128, 16384], F32))
    q1 = mes.enter_context(nc.sbuf_tensor("q1", [128, 1024], F32))
    q2 = mes.enter_context(nc.sbuf_tensor("q2", [128, 1024], F32))
    q1T = mes.enter_context(nc.sbuf_tensor("q1T", [128, 1024], F32))
    q2T = mes.enter_context(nc.sbuf_tensor("q2T", [128, 1024], F32))
    idf = mes.enter_context(nc.sbuf_tensor("idf_s", [128, 128], F32))
    msb = mes.enter_context(nc.sbuf_tensor("msb", [128, 1024], FP16))
    pT = mes.enter_context(nc.psum_tensor("pT", [128, 128], F32))
    mps = ExitStack()
    p13 = mps.enter_context(nc.psum_tensor("p13", [128, 1024], F32))
    p2p = mps.enter_context(nc.psum_tensor("p2p", [128, 1024], F32))
    p4p = mps.enter_context(nc.psum_tensor("p4p", [128, 1024], F32))

    mload_pairs = [
        dma("sync", w32[:, 0:8192], wr_d.rearrange("(dc p) k -> p dc k", p=128)),
        dma("sync", w32[:, 8192:16384], wi_d.rearrange("(dc p) k -> p dc k", p=128)),
        dma("sync", cs1[:], cs1_d.rearrange("c (dc p) j -> p c dc j", p=128)),
        dma("sync", cful[:, 0:8192], cfc_d.rearrange("(dc p) o -> p dc o", p=128)),
        dma("sync", cful[:, 8192:16384], cfs_d.rearrange("(dc p) o -> p dc o", p=128)),
        dma("sync", idf[:], idf_d)]

    def W32(which, dc, lo, hi):
        return w32[:, which * 8192 + dc * 1024 + lo: which * 8192 + dc * 1024 + hi]

    def CS1(which, dc):
        return cs1[:, which * 1024 + dc * 128: which * 1024 + dc * 128 + 128]

    def CFUL(which, dc, lo, hi):
        return cful[:, which * 8192 + dc * 1024 + lo: which * 8192 + dc * 1024 + hi]

    tk = None
    for half in range(2):
        lo, hi = half * 512, half * 512 + 512
        pairs = [(0, 0)] * 0
        n = 0
        for (wch, cch) in ((0, 0), (1, 1)):
            for dc in range(8):
                tk = p.emit("tensor",
                            lambda e, cch=cch, wch=wch, dc=dc, lo=lo, hi=hi, n=n: e.matmul(
                                p13[:, lo:hi], CS1(cch, dc), W32(wch, dc, lo, hi),
                                start=(n == 0), stop=(n == 15)),
                            waits=(mload_pairs if n == 0 else []),
                            inc=cP if n == 15 else None)
                n += 1
        for dc in range(8):
            tk = p.emit("tensor",
                        lambda e, dc=dc, lo=lo, hi=hi: e.matmul(
                            p2p[:, lo:hi], CS1(1, dc), W32(0, dc, lo, hi),
                            start=(dc == 0), stop=(dc == 7)),
                        inc=cP if dc == 7 else None)
        for dc in range(8):
            tk = p.emit("tensor",
                        lambda e, dc=dc, lo=lo, hi=hi: e.matmul(
                            p4p[:, lo:hi], CS1(0, dc), W32(1, dc, lo, hi),
                            start=(dc == 0), stop=(dc == 7)),
                        inc=cP if dc == 7 else None)
    t_stage1 = tk if tk is not None else 0
    t_stage1 = p.val(cP)
    tv = p.emit("vector", lambda e: e.tensor_copy(q1[:], p13[:]), waits=[(cP, t_stage1)], inc=cV)
    tv = p.emit("vector", lambda e: e.tensor_copy(q2T[:], p4p[:]), inc=cV)
    tv = p.emit("vector", lambda e: e.tensor_tensor(q2[:], p2p[:], q2T[:], ALU.subtract), inc=cV)
    t_qev = p.val(cV)
    for src, dst in ((q1, q1T), (q2, q2T)):
        for kc in range(8):
            tt = p.emit("tensor",
                        lambda e, src=src, kc=kc: e.transpose(
                            pT[:], src[:, kc * 128: kc * 128 + 128], idf[:]),
                        waits=[(cV, p.val(cV))], inc=cP)
            p.emit("vector",
                   lambda e, dst=dst, kc=kc: e.tensor_copy(dst[:, kc * 128: kc * 128 + 128], pT[:]),
                   waits=[(cP, tt)], inc=cV)
    t_qT = p.val(cV)
    mps.close()
    pM = mes.enter_context(nc.psum_tensor("pM", [128, 1024], F32))
    n = 0
    for half in range(2):
        lo, hi = half * 512, half * 512 + 512
        n = 0
        for which, qt in ((0, q1T), (1, q2T)):
            for kc in range(8):
                tk = p.emit("tensor",
                            lambda e, which=which, qt=qt, kc=kc, lo=lo, hi=hi, n=n: e.matmul(
                                pM[:, lo:hi], qt[:, kc * 128: kc * 128 + 128],
                                CFUL(which, kc, lo, hi),
                                start=(n == 0), stop=(n == 15)),
                            waits=([(cV, t_qT)] if n == 0 else []),
                            inc=cP if n == 15 else None)
                n += 1
    t_pm = p.val(cP)
    t_msb = p.emit("vector", lambda e: e.tensor_scalar(msb[:], pM[:], 1.0 / D, None, ALU.mult),
                   waits=[(cP, t_pm)], inc=cV)
    t_mg = dma("sync", mg_in.ap(), msb[:], None, waits=[(cV, t_msb)])
    p.emit("gpsimd", lambda e: e.collective_compute(
        "AllGather", ALU.bypass, replica_groups=[list(range(8))],
        ins=[mg_in.ap().opt()], outs=[mg_out.ap().opt()]),
        waits=[t_mg], inc=ccS)
    t_mcc = p.val(ccS)
    mes.close()

    wqb = es.enter_context(nc.sbuf_tensor("wqb", [128, 8192], BF16))
    wkb = es.enter_context(nc.sbuf_tensor("wkb", [128, 8192], BF16))
    wvb = es.enter_context(nc.sbuf_tensor("wvb", [128, 8192], BF16))
    wob = es.enter_context(nc.sbuf_tensor("wob", [128, 8192], BF16))
    msb2 = es.enter_context(nc.sbuf_tensor("msb2", [128, 8192], FP16))
    xTb = es.enter_context(nc.sbuf_tensor("xTb", [128, 8192], BF16))
    kT = es.enter_context(nc.sbuf_tensor("kT", [128, 8192], BF16))
    qT = es.enter_context(nc.sbuf_tensor("qT", [128, 8192], BF16))
    vhat = es.enter_context(nc.sbuf_tensor("vhat", [128, 8448], BF16))
    ktok = [es.enter_context(nc.sbuf_tensor(f"ktok{i}", [128, 1024], BF16)) for i in range(3)]
    bst = es.enter_context(nc.sbuf_tensor("bst", [128, 8 * 520], BF16))
    run = es.enter_context(nc.sbuf_tensor("run", [128, 520], F32))
    gsl = es.enter_context(nc.sbuf_tensor("gsl", [128, 1040], F32))
    pfA = es.enter_context(nc.sbuf_tensor("pfA", [128, 520], F32))
    pfb16 = es.enter_context(nc.sbuf_tensor("pfb16", [128, 520], BF16))
    amask = [es.enter_context(nc.sbuf_tensor(f"amask{i}", [128, 512], BF16)) for i in range(2)]
    o_sb = [es.enter_context(nc.sbuf_tensor(f"o_sb{i}", [128, 1024], BF16)) for i in range(2)]
    oT = es.enter_context(nc.sbuf_tensor("oT", [128, 1024], BF16))
    x_tok = [es.enter_context(nc.sbuf_tensor(f"x_tok{i}", [128, 1024], F32)) for i in range(2)]
    t1 = es.enter_context(nc.sbuf_tensor("t1", [128, 1024], F32))
    sq = es.enter_context(nc.sbuf_tensor("sq", [128, 512], BF16))

    x1h = es.enter_context(nc.sbuf_tensor("x1h", [128, 1024], FP16))
    x1T = [es.enter_context(nc.sbuf_tensor(f"x1T{i}", [128, 1024], FP16)) for i in range(2)]
    out_sb = es.enter_context(nc.sbuf_tensor("out_sb", [128, 1024], F32))
    eS = [es.enter_context(nc.sbuf_tensor(f"eS{i}", [128, 512], BF16)) for i in range(2)]
    rS = [es.enter_context(nc.sbuf_tensor(f"rS{i}", [128, 512], BF16)) for i in range(2)]
    stat = es.enter_context(nc.sbuf_tensor("stat", [128, 16], F32))
    rden = es.enter_context(nc.sbuf_tensor("rden", [128, 8], F32))
    maskc = es.enter_context(nc.sbuf_tensor("maskc_s", [128, 128], BF16))
    ones16 = es.enter_context(nc.sbuf_tensor("ones16_s", [1, 128], FP16))
    id16 = es.enter_context(nc.sbuf_tensor("id16_s", [128, 128], FP16))
    idb = es.enter_context(nc.sbuf_tensor("idb_s", [128, 128], BF16))
    bias16 = es.enter_context(nc.sbuf_tensor("bias16", [1, 1024], FP16))
    mask8 = es.enter_context(nc.sbuf_tensor("mask8_s", [128, 8], F32))

    G = [es.enter_context(nc.psum_tensor(f"G{i}", [128, 512], F32)) for i in range(5)]
    ps_state = [es.enter_context(nc.psum_tensor(f"st{i}", [128, 512], F32)) for i in range(2)]
    ps_z = es.enter_context(nc.psum_tensor("stz", [128, 512], F32))
    G = G + [ps_state[0], ps_state[1], ps_z]

    m_guards = [(cP, t_pm), (cV, t_msb), t_mg]
    wload_pairs = [
        dma("gpsimd", wqb[:], wq_d.rearrange("(kc p) o -> p kc o", p=128), None, waits=m_guards),
        dma("gpsimd", wkb[:], wk_d.rearrange("(kc p) o -> p kc o", p=128), None, waits=m_guards),
        dma("gpsimd", wvb[:], wv_d.rearrange("(kc p) o -> p kc o", p=128), None, waits=m_guards),
        dma("gpsimd", wob[:], wo_d.rearrange("(kc p) o -> p kc o", p=128), None, waits=m_guards)]
    dma("sync", maskc[:], maskc_d, None, waits=m_guards)
    dma("sync", ones16[:], ones16_d, None, waits=m_guards)
    dma("sync", id16[:], id16_d, None, waits=m_guards)
    dma("sync", idb[:], idb_d, None, waits=m_guards)
    t_cload = dma("sync", mask8[:], mask8_d, None, waits=m_guards)
    t_bias = dma("gpsimd", bias16[:], bias_d, None, waits=m_guards)
    t_init = p.emit("vector", lambda e: e.memset(vhat[:], 1.0), inc=cV)
    t_mld = dma("sync", msb2[:], mg_out.ap().rearrange("(kc p) o -> p kc o", p=128), None,
                waits=[(ccS, t_mcc)])

    bank_free = {i: 0 for i in range(8)}

    ROT = [2, 3, 4]

    def bf_wait(i):
        v = bank_free[i]
        if isinstance(v, tuple):
            return ({"cG": cG, "cA": cA, "cV": cV}[v[0]], v[1])
        return (cV, v)
    amask_free = {0: 0, 1: 0}
    prev_ow = {0: None, 1: None}
    prev_pet = {0: 0, 1: 0}
    prev_or = [None]
    prev_fw = {0: None, 1: None}
    prev_wo = [0]
    prev_ffn = [0]
    prev_xt_use = {0: 0, 1: 0}
    prev_ln2_apply = [0]
    rot = {"n": 0}
    fmap_state = {"n": 0, "last": [0, 0]}

    def fmap_evac(psrc, dst_ap, wtok):
        sl_ = fmap_state["n"] % 2
        fmap_state["n"] += 1
        ta = p.emit("scalar", lambda e, psrc=psrc, sl_=sl_: e.activation(eS[sl_][:], psrc[:], AF.Exp),
                    waits=list(wtok) + [(cV, fmap_state["last"][sl_])], inc=cA)
        ta = p.emit("scalar", lambda e, psrc=psrc, sl_=sl_: e.activation(rS[sl_][:], psrc[:], AF.Relu),
                    inc=cA)
        tv = p.emit("vector", lambda e, dst_ap=dst_ap, sl_=sl_: e.scalar_tensor_tensor(
            dst_ap, eS[sl_][:], 1.0, rS[sl_][:], ALU.min, ALU.add),
            waits=[(cA, ta)], inc=cV)
        fmap_state["last"][sl_] = tv
        return tv

    ln_state = {"last_act": 0}

    def ln_from_sum(src_sb, dst_ap, col, dst_waits=(), sum_act=None):
        tv = p.emit("vector", lambda e, col=col: e.tensor_tensor(
            stat[:, col:col + 1], stat[:, col + 6:col + 7], stat[:, col + 7:col + 8], ALU.add),
            waits=([(cA, sum_act)] if sum_act else []), inc=cV)
        tv = p.emit("vector", lambda e, col=col: e.tensor_scalar(
            stat[:, col + 1:col + 2], stat[:, col:col + 1], 1.0 / D, None, ALU.mult), inc=cV)
        tv = p.emit("vector", lambda e, col=col: e.tensor_scalar(
            stat[:, col + 2:col + 3], stat[:, col:col + 1], -1.0 / D, None, ALU.mult), inc=cV)
        ta = p.emit("scalar", lambda e, src_sb=src_sb, col=col: e.activation(
            sq[:], src_sb[:, 0:512], AF.Square, bias=stat[:, col + 2:col + 3],
            accum_out=stat[:, col + 3:col + 4]),
            waits=[(cV, tv)], inc=cA)
        ta = p.emit("scalar", lambda e, src_sb=src_sb, col=col: e.activation(
            sq[:], src_sb[:, 512:1024], AF.Square, bias=stat[:, col + 2:col + 3],
            accum_out=stat[:, col + 4:col + 5]), inc=cA)
        tv = p.emit("vector", lambda e, col=col: e.tensor_tensor(
            stat[:, col + 3:col + 4], stat[:, col + 3:col + 4],
            stat[:, col + 4:col + 5], ALU.add), waits=[(cA, ta)], inc=cV)
        tv = p.emit("vector", lambda e, col=col: e.tensor_scalar(
            stat[:, col + 3:col + 4], stat[:, col + 3:col + 4], 1.0 / D, EPS_LN,
            ALU.mult, ALU.add), inc=cV)
        ta = p.emit("scalar", lambda e, col=col: e.activation(
            stat[:, col + 4:col + 5], stat[:, col + 3:col + 4], AF.Sqrt),
            waits=[(cV, tv)], inc=cA)
        ln_state["last_act"] = ta
        tv = p.emit("vector", lambda e, col=col: e.reciprocal(
            stat[:, col + 5:col + 6], stat[:, col + 4:col + 5]), waits=[(cA, ta)], inc=cV)
        tv = p.emit("vector", lambda e, src_sb=src_sb, dst_ap=dst_ap, col=col: e.tensor_scalar(
            dst_ap, src_sb[:], stat[:, col + 1:col + 2], stat[:, col + 5:col + 6],
            ALU.subtract, ALU.mult), waits=[w for w in dst_waits if w], inc=cV)
        return tv

    NB = int(os.environ.get("KMK_NB", str(B)))

    kw_pairs = {}
    xTp = {}
    kT_half_tok = {}
    vhat_tok = {}
    bst_tok = {}
    bsteff_tok = {}
    prefix_tok = {}
    qproj_tok = {}
    waves_done = {}
    A3_done = {}
    gather_last_stt = [0]
    gather_gld = [None, None]
    prev_state_evac = [0]
    prev_scc = [0]
    prev_sgw = [None]
    prev_kr = {i: None for i in range(3)}
    prev_state_mm = {i: 0 for i in range(3)}
    kr_tok = {}
    state_q_free = [0]

    def emit_xprep(b):
        t_cast = dma("gpsimd", scr_xb.ap(), xs_d[b], None,
                     waits=list(xTp.get(b - 1, [])))
        pairs = []
        for fc in range(8):
            pairs.append(dma("sync", xTb[:, fc * 1024:(fc + 1) * 1024],
                             scr_xb.ap()[:, fc * 128:(fc + 1) * 128], None,
                             waits=[t_cast, (cV, qproj_tok.get((b - 1, 1), 0))] + m_guards,
                             transpose=True))
        xTp[b] = pairs

    def emit_ktproj(b, half, pool=None):
        guards = []
        if b > 0:
            guards.append((cP, A3_done.get((b - 1, half), 0)))
            for cc2 in range(half * 4, half * 4 + 4):
                pr = kw_pairs.get((b - 1, cc2))
                if pr:
                    guards.append(pr)
        for oc in range(8):
            if pool is not None:
                g = pool[oc % len(pool)]
            else:
                g = ROT[rot["n"] % 3]
                rot["n"] += 1
            for kc in range(8):
                tk = p.emit("tensor",
                            lambda e, g=g, kc=kc, oc=oc, half=half: e.matmul(
                                G[g][:],
                                wkb[:, kc * 1024 + oc * 128: kc * 1024 + oc * 128 + 128],
                                xTb[:, kc * 1024 + half * 512: kc * 1024 + half * 512 + 512],
                                start=(kc == 0), stop=(kc == 7)),
                            waits=([bf_wait(g)] + xTp[b] + wload_pairs
                                   if kc == 0 else []),
                            inc=cP if kc == 7 else None)
            tv = fmap_evac(G[g], kT[:, oc * 1024 + half * 512: oc * 1024 + half * 512 + 512],
                           [(cP, tk)] + guards)
            bank_free[g] = tv
        kT_half_tok[(b, half)] = p.val(cV)

    def emit_ktok(b, c):
        sl = c % 3
        t_kw = dma("gpsimd", scr_k[sl].ap().rearrange("(fc p) t -> p fc t", p=128),
                   kT[:].rearrange("p (fc t) -> p fc t", t=1024)[:, :, c * 128:(c + 1) * 128],
                   None, waits=[(cV, kT_half_tok[(b, 0 if c < 4 else 1)]), prev_kr[sl]])
        kw_pairs[(b, c)] = t_kw
        t_kr = dma("sync", ktok[sl][:], scr_k[sl].ap(), None,
                   waits=[t_kw, (cP, prev_state_mm[sl])], transpose=True)
        prev_kr[sl] = t_kr
        kr_tok[(b, c)] = t_kr

    def emit_kchunk(b, c):
        g0 = ROT[rot["n"] % 3]
        rot["n"] += 1
        g1 = ROT[rot["n"] % 3]
        rot["n"] += 1
        for kc in range(8):
            for half2, g in ((0, g0), (1, g1)):
                tk = p.emit("tensor",
                            lambda e, g=g, kc=kc, tt=c, half2=half2: e.matmul(
                                G[g][:],
                                xTb[:, kc * 1024 + tt * 128: kc * 1024 + tt * 128 + 128],
                                wvb[:, kc * 1024 + half2 * 512: kc * 1024 + half2 * 512 + 512],
                                start=(kc == 0), stop=(kc == 7)),
                            waits=([bf_wait(g0), bf_wait(g1), (cV, t_init)] + xTp[b]
                                   if kc == 0 and half2 == 0 else []),
                            inc=cP if kc == 7 and half2 == 1 else None)
        for half2, g in ((0, g0), (1, g1)):
            tv = p.emit("scalar",
                        lambda e, g=g, tt=c, half2=half2: e.activation(
                            vhat[:, tt * 1056 + half2 * 528: tt * 1056 + half2 * 528 + 528]
                            .rearrange("p (h cc) -> p h cc", cc=66)[:, :, 0:64],
                            G[g][:].rearrange("p (h cc) -> p h cc", cc=64), AF.Copy),
                        waits=[(cP, tk), (cP, waves_done.get((b - 1, c), 0))], inc=cA)
            bank_free[g] = ("cA", tv)
        vhat_tok[(b, c)] = ("cA", p.val(cA))
        sl = c % 3
        t_kr = kr_tok[(b, c)]
        if c == 0:
            p.emit("vector", lambda e: e.memset(run[:], 0.0),
                   waits=[prev_sgw[0]], inc=cV)
        for j in range(8):
            ps = ps_state[j // 4]
            tk = p.emit("tensor",
                        lambda e, ps=ps, j=j, sl=sl, c=c: e.matmul(
                            ps[:, (j % 4) * 128:(j % 4) * 128 + 128],
                            ktok[sl][:, j * 128:(j + 1) * 128],
                            vhat[:, c * 1056 + (2 * j) * 66: c * 1056 + (2 * j) * 66 + 132]
                            .rearrange("p (h q) -> p h q", q=66)[:, :, 0:64],
                            start=((j % 4) == 0), stop=((j % 4) == 3)),
                        waits=([t_kr, (cA, vhat_tok[(b, c)][1]), (cV, prev_state_evac[0]),
                                bf_wait(5), bf_wait(6)]
                               if j == 0 else []),
                        inc=None)
        for j in range(8):
            tk = p.emit("tensor",
                        lambda e, j=j, sl=sl, c=c: e.matmul(
                            ps_z[:, j:j + 1],
                            ktok[sl][:, j * 128:(j + 1) * 128],
                            vhat[:, c * 1056 + (2 * j) * 66 + 64: c * 1056 + (2 * j) * 66 + 65],
                            start=(j == 0), stop=(j == 7)),
                        waits=([bf_wait(7)] if j == 0 else []),
                        inc=cP if j == 7 else None)
        prev_state_mm[sl] = tk
        stt_last = [0, 0]
        for bank_i in range(2):
            for parity in range(2):
                stt_last[bank_i] = p.emit("vector", lambda e, bank_i=bank_i, parity=parity:
                       e.scalar_tensor_tensor(
                           run[parity * 64:parity * 64 + 64,
                               bank_i * 260: bank_i * 260 + 260]
                           .rearrange("p (j q) -> p j q", q=65)[:, :, 0:64],
                           ps_state[bank_i][parity * 64:parity * 64 + 64, :]
                           .rearrange("p (j q) -> p j q", q=128)[:, :, parity * 64:parity * 64 + 64],
                           1.0,
                           run[parity * 64:parity * 64 + 64,
                               bank_i * 260: bank_i * 260 + 260]
                           .rearrange("p (j q) -> p j q", q=65)[:, :, 0:64],
                           ALU.mult, ALU.add),
                       waits=[(cP, tk)], inc=cV)
        bank_free[5] = ("cV", stt_last[0])
        bank_free[6] = ("cV", stt_last[1])
        tvz = p.emit("vector", lambda e: e.scalar_tensor_tensor(
            run[:].rearrange("p (j q) -> p j q", q=65)[:, :, 64:65],
            ps_z[:, 0:8].unsqueeze(2), 1.0,
            run[:].rearrange("p (j q) -> p j q", q=65)[:, :, 64:65],
            ALU.mult, ALU.add), inc=cV)
        bank_free[7] = tvz
        tv = p.emit("vector", lambda e, c=c: e.tensor_copy(
            bst[:, c * 520: c * 520 + 520], run[:]),
            waits=[(cP, waves_done.get((b - 1, min(c + 1, NCH - 1)), 0))], inc=cV)
        bst_tok[(b, c)] = tv
        prev_state_evac[0] = tv

    def emit_gather(b):
        t_sgw = dma("sync", sg_in.ap(), run[:], None,
                    waits=[(cV, prev_state_evac[0]), (ccS, prev_scc[0])])
        prev_sgw[0] = t_sgw
        p.emit("gpsimd", lambda e: e.collective_compute(
            "AllGather", ALU.bypass, replica_groups=[list(range(8))],
            ins=[sg_in.ap().opt()], outs=[sg_out.ap().opt()]),
            waits=[t_sgw] + [x for x in gather_gld if x], inc=ccS)
        t_scc = p.val(ccS)
        prev_scc[0] = t_scc
        p.emit("vector", lambda e: e.memset(pfA[:], 0.0),
               waits=[(cV, gather_last_stt[0])], inc=cV)
        stt_tok = [0] * 8
        for r in range(8):
            t_gld = dma("sync", gsl[:, (r % 2) * 520:(r % 2) * 520 + 520],
                        sg_out.ap()[r * 128:(r + 1) * 128, :], None,
                        waits=[(ccS, t_scc), (cV, stt_tok[r - 2] if r >= 2 else 0)])
            gather_gld[r % 2] = t_gld
            stt_tok[r] = p.emit("vector", lambda e, r=r: e.scalar_tensor_tensor(
                pfA[:], gsl[:, (r % 2) * 520:(r % 2) * 520 + 520], mask8[:, r:r + 1], pfA[:],
                ALU.mult, ALU.add),
                waits=[t_gld, t_cload], inc=cV)
        gather_last_stt[0] = stt_tok[7]
        prefix_tok[b] = p.emit("vector", lambda e: e.tensor_copy(pfb16[:], pfA[:]), inc=cV)
        for c2 in range(NCH - 1):
            bsteff_tok[(b, c2)] = p.emit("vector", lambda e, c2=c2: e.tensor_tensor(
                bst[:, c2 * 520: c2 * 520 + 520],
                bst[:, c2 * 520: c2 * 520 + 520], pfb16[:], ALU.add),
                waits=[(cV, bst_tok[(b, c2)])], inc=cV)

    qT_guard = {}
    y_guard = {}
    chunk_info = {}

    def emit_qproj(b, g, pool=None):
        for oc in range(8):
            if pool is not None:
                gb = pool[(oc + 3) % len(pool)]
            else:
                gb = ROT[rot["n"] % 3]
                rot["n"] += 1
            for kc in range(8):
                tk = p.emit("tensor",
                            lambda e, gb=gb, kc=kc, oc=oc, g=g: e.matmul(
                                G[gb][:],
                                wqb[:, kc * 1024 + oc * 128: kc * 1024 + oc * 128 + 128],
                                xTb[:, kc * 1024 + g * 512: kc * 1024 + g * 512 + 512],
                                start=(kc == 0), stop=(kc == 7)),
                            waits=([bf_wait(gb)] if kc == 0 else []),
                            inc=cP if kc == 7 else None)
            tv = fmap_evac(G[gb], qT[:, g * 4096 + oc * 512: g * 4096 + oc * 512 + 512],
                           [(cP, tk), (cP, qT_guard.get((b - 1, g), 0))])
            bank_free[gb] = tv
        qproj_tok[(b, g)] = p.val(cV)

    def emit_waves(b, c):
        g = c // 4
        tg = (c % 4) * 128
        sl = c % 2
        t_xt = dma("sync", x_tok[sl][:], xs_d[b, c * 128:(c + 1) * 128, :], None,
                   waits=[(cV, prev_xt_use[sl])])
        div_tok = 0
        wave_bank = {}
        wave_am = {}

        def emit_A(w):
            half = w // 2
            parity = w % 2
            par = parity * 64
            heads = [half * 8 + parity + 2 * jj for jj in range(4)]
            ga = ROT[rot["n"] % 3]
            rot["n"] += 1
            wave_bank[w] = ga
            for i in range(4):
                h = heads[i]
                fc = h // 2
                tk = p.emit("tensor",
                            lambda e, ga=ga, i=i, fc=fc, par=par, tg=tg, c=c, g=g: e.matmul(
                                G[ga][:, i * 128:(i + 1) * 128],
                                kT[par:par + 64, fc * 1024 + c * 128: fc * 1024 + c * 128 + 128],
                                qT[par:par + 64, g * 4096 + fc * 512 + tg: g * 4096 + fc * 512 + tg + 128],
                                start=True, stop=True,
                                tile_position=(par, 0)),
                            waits=([bf_wait(ga), (cV, qproj_tok[(b, g)])]
                                   if i == 0 else []),
                            inc=cP if i == 3 else None)
            wave_am[w] = p.emit("vector", lambda e, ga=ga, w=w: e.tensor_tensor(
                amask[w % 2][:].rearrange("p (i t) -> p i t", t=128),
                G[ga][:].rearrange("p (i t) -> p i t", t=128),
                maskc[:].unsqueeze(1).broadcast_to([128, 4, 128]),
                ALU.mult),
                waits=[(cP, tk), (cP, amask_free[w % 2])], inc=cV)
            return tk

        def emit_num(w):
            nonlocal div_tok
            half = w // 2
            parity = w % 2
            par = parity * 64
            heads = [half * 8 + parity + 2 * jj for jj in range(4)]
            gnum = wave_bank[w]
            t_am = wave_am[w]
            for i in range(4):
                h = heads[i]
                j = (h % 8) // 2
                scol = (h // 8) * 260 + j * 65
                p.emit("tensor",
                       lambda e, gnum=gnum, i=i, h=h, w=w, c=c: e.matmul(
                           G[gnum][:, i * 65: i * 65 + 65],
                           amask[w % 2][:, i * 128:(i + 1) * 128],
                           vhat[:, c * 1056 + h * 66: c * 1056 + h * 66 + 65],
                           start=(i == 0), stop=False),
                       waits=([(cV, t_am)]))
                state_rhs = (pfb16[par:par + 64, scol: scol + 65] if c == 0 else
                             bst[par:par + 64, (c - 1) * 520 + scol: (c - 1) * 520 + scol + 65])
                tk = p.emit("tensor",
                            lambda e, gnum=gnum, i=i, par=par, tg=tg, h=h, state_rhs=state_rhs, g=g: e.matmul(
                                G[gnum][:, i * 65: i * 65 + 65],
                                qT[par:par + 64, g * 4096 + (h // 2) * 512 + tg: g * 4096 + (h // 2) * 512 + tg + 128],
                                state_rhs,
                                start=False, stop=(i == 3)),
                            waits=[(cV, prefix_tok[b] if c == 0 else bsteff_tok[(b, c - 1)])],
                            inc=cP if i == 3 else None)
            amask_free[w % 2] = tk
            p.emit("vector", lambda e, gnum=gnum: e.tensor_scalar(
                rden[:, 0:4],
                G[gnum][:, 0:260].rearrange("p (i cq) -> p i cq", cq=65)[:, :, 64:65].squeeze(2),
                EPS_ATTN, None, ALU.add),
                waits=[(cP, tk)], inc=cV)
            p.emit("vector", lambda e: e.reciprocal(rden[:, 4:8], rden[:, 0:4]), inc=cV)
            div_tok = p.emit("vector", lambda e, gnum=gnum, half=half, parity=parity, sl=sl: e.tensor_tensor(
                o_sb[sl][:].rearrange("p (hf jj pa cc) -> p hf jj pa cc", hf=2, jj=4, pa=2)[:, half, :, parity, :],
                G[gnum][:, 0:260].rearrange("p (i cq) -> p i cq", cq=65)[:, :, 0:64],
                rden[:, 4:8].unsqueeze(2).broadcast_to([128, 4, 64]),
                ALU.mult),
                waits=[(cP, prev_pet[sl])], inc=cV)
            bank_free[gnum] = div_tok

        emit_A(0)
        emit_A(1)
        last_A = 0
        for w in range(4):
            emit_num(w)
            if w + 2 < 4:
                last_A = emit_A(w + 2)
        waves_done[(b, c)] = p.val(cP)
        if c % 4 == 3:
            A3_done[(b, c // 4)] = p.val(cP)
        chunk_info[(b, c)] = (div_tok, t_xt, sl, None)

    def emit_tail_a(c, b):
        div_tok, t_xt, sl, _unused = chunk_info[(b, c)]
        x1Ts = x1T[c % 2]
        gt = ROT[rot["n"] % 3]
        rot["n"] += 1
        gtv = G[gt][:].bitcast(BF16)
        for fc in range(8):
            tk = p.emit("tensor",
                        lambda e, gt=gt, gtv=gtv, fc=fc, sl=sl: e.transpose(
                            gtv[:, fc * 128:(fc + 1) * 128],
                            o_sb[sl][:, fc * 128:(fc + 1) * 128], idb[:]),
                        waits=([bf_wait(gt), (cV, div_tok), (cP, prev_wo[0])]
                               if fc == 0 else []),
                        inc=cP if fc == 7 else None)
        prev_pet[sl] = tk
        t_oT = p.emit("vector", lambda e, gtv=gtv: e.tensor_copy(oT[:], gtv[:]),
                      waits=[(cP, tk)], inc=cV)
        bank_free[gt] = t_oT
        for fc in range(8):
            for half2 in range(2):
                gb = half2
                tk = p.emit("tensor",
                            lambda e, gb=gb, fc=fc, half2=half2: e.matmul(
                                G[gb][:], oT[:, fc * 128:(fc + 1) * 128],
                                wob[:, fc * 1024 + half2 * 512: fc * 1024 + half2 * 512 + 512],
                                start=(fc == 0), stop=(fc == 7)),
                            waits=([bf_wait(0), bf_wait(1), (cV, t_oT)]
                                   if fc == 0 and half2 == 0 else []),
                            inc=cP if fc == 7 and half2 == 1 else None)
        t_wo = p.val(cP)
        prev_wo[0] = t_wo
        tv = p.emit("vector", lambda e, sl=sl: e.scalar_tensor_tensor(
            t1[:, 0:512], G[0][:], 1.0, x_tok[sl][:, 0:512], ALU.mult, ALU.add,
            accum_out=stat[:, 6:7]),
            waits=[(cP, t_wo), t_xt, (cA, ln_state["last_act"])], inc=cV)
        tv = p.emit("vector", lambda e, sl=sl: e.scalar_tensor_tensor(
            t1[:, 512:1024], G[1][:], 1.0, x_tok[sl][:, 512:1024], ALU.mult, ALU.add,
            accum_out=stat[:, 7:8]), inc=cV)
        bank_free[0] = tv
        bank_free[1] = tv
        prev_xt_use[sl] = tv
        t_x1 = ln_from_sum(t1, x1h[:], 0, dst_waits=[prev_fw[0], prev_fw[1]])
        t_fw = dma("gpsimd", scr_f[sl].ap(), x1h[:], None, waits=[(cV, t_x1)])
        prev_fw[sl] = t_fw
        fr_pairs = []
        for fc in range(8):
            fr_pairs.append(dma("sync", x1Ts[:, fc * 128:(fc + 1) * 128],
                            scr_f[sl].ap()[:, fc * 128:(fc + 1) * 128], None,
                            waits=[t_fw, (cP, prev_ffn[0])], transpose=True))
        chunk_info[(b, c, "a")] = fr_pairs

    def emit_tail_b(c, b):
        div_tok, t_xt, sl, _t_ow = chunk_info[(b, c)]
        fr_pairs = chunk_info[(b, c, "a")]
        x1Ts = x1T[c % 2]
        for fc in range(8):
            for half2 in range(2):
                gb = half2
                p.emit("tensor",
                       lambda e, gb=gb, fc=fc, half2=half2, x1Ts=x1Ts: e.matmul(
                           G[gb][:], x1Ts[:, fc * 128:(fc + 1) * 128],
                           msb2[:, fc * 1024 + half2 * 512: fc * 1024 + half2 * 512 + 512],
                           start=(fc == 0), stop=False),
                       waits=([bf_wait(0), bf_wait(1), t_mld] + fr_pairs
                              if fc == 0 and half2 == 0 else []))
        for half2 in range(2):
            gb = half2
            p.emit("tensor",
                   lambda e, gb=gb, half2=half2: e.matmul(
                       G[gb][:], ones16[:],
                       bias16[0:1, half2 * 512: half2 * 512 + 512],
                       start=False, stop=False),
                   waits=[t_bias])
        for fq in range(4):
            for half2 in range(2):
                gb = half2
                fc = half2 * 4 + fq
                tk = p.emit("tensor",
                            lambda e, gb=gb, fq=fq, fc=fc, x1Ts=x1Ts: e.matmul(
                                G[gb][:, fq * 128:(fq + 1) * 128],
                                x1Ts[:, fc * 128:(fc + 1) * 128], id16[:],
                                start=False, stop=(fq == 3)),
                            inc=cP if fq == 3 and half2 == 1 else None)
        t_ffn = p.val(cP)
        prev_ffn[0] = t_ffn
        ta = p.emit("scalar", lambda e: e.activation(
            out_sb[:, 0:512], G[0][:], AF.Copy, accum_out=stat[:, 14:15]),
            waits=[(cP, t_ffn), (cV, prev_ln2_apply[0]), y_guard.get(0)], inc=cA)
        ta = p.emit("scalar", lambda e: e.activation(
            out_sb[:, 512:1024], G[1][:], AF.Copy, accum_out=stat[:, 15:16]), inc=cA)
        bank_free[0] = ("cA", ta)
        bank_free[1] = ("cA", ta)
        t_out = ln_from_sum(out_sb, out_sb[:], 8, dst_waits=[], sum_act=ta)
        prev_ln2_apply[0] = t_out
        y_guard[0] = dma("gpsimd", y_d[b, c * 128:(c + 1) * 128, :], out_sb[:], None,
                         waits=[(cV, t_out)])

    emit_xprep(0)
    PB5 = [2, 3, 4, 5, 6]
    emit_ktproj(0, 0, pool=PB5)
    for c in range(3):
        emit_ktok(0, c)
    emit_ktproj(0, 1, pool=PB5)
    emit_qproj(0, 0, pool=PB5)
    for c in range(NCH):
        emit_kchunk(0, c)
        if c + 3 < NCH:
            emit_ktok(0, c + 3)
    emit_gather(0)
    seq = [(b, c) for b in range(NB) for c in range(NCH)]
    emit_qproj(0, 1)
    for gi, (b, c) in enumerate(seq):
        if c <= 1:
            if gi >= 1:
                emit_tail_a(seq[gi - 1][1], seq[gi - 1][0])
            if gi >= 2:
                emit_tail_b(seq[gi - 2][1], seq[gi - 2][0])
            emit_waves(b, c)
        else:
            emit_waves(b, c)
            if gi >= 1:
                emit_tail_a(seq[gi - 1][1], seq[gi - 1][0])
            if gi >= 2:
                emit_tail_b(seq[gi - 2][1], seq[gi - 2][0])
        if c == 3:
            qT_guard[(b, 0)] = p.val(cP)
        elif c == 7:
            qT_guard[(b, 1)] = p.val(cP)
        if b + 1 < NB:
            if c == 0:
                emit_xprep(b + 1)
            elif c == 4:
                emit_ktproj(b + 1, 0, pool=PB5)
                for c2 in range(3):
                    emit_ktok(b + 1, c2)
                emit_qproj(b + 1, 0, pool=PB5)
            elif c == 5:
                emit_kchunk(b + 1, 0)
                emit_ktok(b + 1, 3)
                emit_kchunk(b + 1, 1)
            elif c == 6:
                emit_kchunk(b + 1, 2)
                emit_kchunk(b + 1, 3)
            elif c == 7:
                emit_ktproj(b + 1, 1)
                for c2 in range(4, 7):
                    emit_ktok(b + 1, c2)
                emit_kchunk(b + 1, 4)
                emit_ktok(b + 1, 7)
                for c2 in range(5, NCH):
                    emit_kchunk(b + 1, c2)
                emit_gather(b + 1)
                emit_qproj(b + 1, 1)
    emit_tail_a(NCH - 1, NB - 1)
    emit_tail_b(NCH - 2, NB - 1)
    emit_tail_b(NCH - 1, NB - 1)

    p.emit("gpsimd", lambda e: e.nop(),
           waits=[(s, p.val(s)) for s in dSs + dAs + dGs])

    block = es.enter_context(nc.Block())
    p.finalize(block)
    es.close()
    return nc


def _host_constants():
    d = np.arange(D, dtype=np.float64)
    ang = 2.0 * np.pi * np.outer(d, d) / D
    C = np.cos(ang)
    Sn = np.sin(ang)
    maskc = np.triu(np.ones((128, 128), np.float32)).astype(BF)
    idf = np.eye(128, dtype=np.float32)
    ones16 = np.ones((1, 128), np.float16)
    id16 = np.eye(128, dtype=np.float16)
    idb = np.eye(128, dtype=np.float32).astype(BF)
    return C, Sn, maskc, idf, ones16, id16, idb


def kernel(L0_attn_x, L0_attn_wq, L0_attn_wk, L0_attn_wv, L0_attn_wo,
           L0_ffn_w_freq_real, L0_ffn_w_freq_imag, L0_ffn_bias):
    x = np.ascontiguousarray(np.asarray(L0_attn_x, dtype=np.float32))
    wq = np.ascontiguousarray(np.asarray(L0_attn_wq, dtype=np.float32))
    wk = np.ascontiguousarray(np.asarray(L0_attn_wk, dtype=np.float32))
    wv = np.ascontiguousarray(np.asarray(L0_attn_wv, dtype=np.float32))
    wo = np.ascontiguousarray(np.asarray(L0_attn_wo, dtype=np.float32))
    wr = np.ascontiguousarray(np.asarray(L0_ffn_w_freq_real, dtype=np.float32))
    wi = np.ascontiguousarray(np.asarray(L0_ffn_w_freq_imag, dtype=np.float32))
    bias = np.ascontiguousarray(np.asarray(L0_ffn_bias, dtype=np.float32)).reshape(1, D)

    C, Sn, maskc, idf, ones16, id16, idb = _host_constants()
    nc = build_program()
    in_maps = []
    for c in range(NCORES):
        rows = slice(c * 128, (c + 1) * 128)
        cs1 = np.stack([np.ascontiguousarray(C[:, rows]).astype(np.float32),
                        np.ascontiguousarray(Sn[:, rows]).astype(np.float32)], 0)
        mask8 = np.zeros((128, 8), np.float32)
        mask8[:, :c] = 1.0
        in_maps.append({
            "xs": np.ascontiguousarray(x[:, c * TPC:(c + 1) * TPC, :]),
            "wq": wq, "wk": wk, "wv": wv, "wo": wo, "wr": wr, "wi": wi,
            "bias": bias,
            "cfc": C.astype(np.float32), "cfs": Sn.astype(np.float32),
            "cs1": cs1, "maskc": maskc, "idf": idf, "ones16": ones16,
            "id16": id16, "idb": idb, "mask8": mask8,
        })
    res = run_bass_kernel_spmd(nc, in_maps, core_ids=list(range(NCORES)))
    out = np.concatenate([np.asarray(res.results[c]["y"]) for c in range(NCORES)], axis=1)
    return out


if __name__ == "__main__":
    nc = build_program()
    print("build ok; instructions:",
          sum(1 for _ in nc.all_instructions()) if hasattr(nc, "all_instructions") else "?")


# version 32
# speedup vs baseline: 1.0055x; 1.0055x over previous
import os
import numpy as np
import ml_dtypes
from contextlib import ExitStack

import concourse.bass as bass
import concourse.mybir as mybir
from concourse.bass_utils import run_bass_kernel_spmd

F32 = mybir.dt.float32
BF16 = mybir.dt.bfloat16
FP16 = mybir.dt.float16
AF = mybir.ActivationFunctionType
ALU = mybir.AluOpType
AX = mybir.AxisListType.X
BF = ml_dtypes.bfloat16

B, S, D = 4, 8192, 1024
NCORES = 8
TPC = S // NCORES
NCH = TPC // 128
NG = TPC // 512
EPS_ATTN = 1e-6
EPS_LN = 1e-5


class Prog:

    ENGINES = ("tensor", "vector", "scalar", "gpsimd", "sync")

    def __init__(self, nc, sems):
        self.nc = nc
        self.sems = sems
        self.streams = {e: [] for e in self.ENGINES}
        self.cnt = {}
        self.last_wait = {e: {} for e in self.ENGINES}
        self.pending = {e: [] for e in self.ENGINES}
        self.auto_self = {}

    def emit(self, engine, fn, waits=(), inc=None, inc_n=1):
        auto = self.auto_self.get(engine)
        waits = list(waits)
        if auto is not None:
            waits.append((auto, self.val(auto)))
            if inc is None:
                inc = auto
        w = []
        for item in list(self.pending[engine]) + list(waits):
            if item is None:
                continue
            sem, val = item
            if val is None or val <= 0:
                continue
            lw = self.last_wait[engine].get(sem.name, 0)
            if val > lw:
                w.append((sem, val))
                self.last_wait[engine][sem.name] = val
        self.pending[engine] = []
        self.streams[engine].append((tuple(w), fn, inc, inc_n))
        if inc is not None:
            self.cnt[inc.name] = self.cnt.get(inc.name, 0) + inc_n
            return self.cnt[inc.name]
        return None

    def val(self, sem):
        return self.cnt.get(sem.name, 0)

    def fence(self):
        snap = [(s, self.val(s)) for s in self.sems]
        for e in self.ENGINES:
            self.pending[e] = list(snap)

    def finalize(self, block):
        for engine in self.ENGINES:
            stream = self.streams[engine]

            def make(engine_stream):
                def body(eng):
                    for waits, fn, inc, inc_n in engine_stream:
                        for semobj, val in waits:
                            eng.wait_ge(semobj, val)
                        instr = fn(eng)
                        if inc is not None:
                            instr.then_inc(inc, inc_n)
                return body
            getattr(block, engine)(make(stream))


def build_program():
    nc = bass.Bass("TRN2", target_bir_lowering=False, debug=False)

    xs_d = nc.dram_tensor("xs", [B, TPC, D], F32, kind="ExternalInput").ap()
    wq_d = nc.dram_tensor("wq", [D, D], F32, kind="ExternalInput").ap()
    wk_d = nc.dram_tensor("wk", [D, D], F32, kind="ExternalInput").ap()
    wv_d = nc.dram_tensor("wv", [D, D], F32, kind="ExternalInput").ap()
    wo_d = nc.dram_tensor("wo", [D, D], F32, kind="ExternalInput").ap()
    wr_d = nc.dram_tensor("wr", [D, D], F32, kind="ExternalInput").ap()
    wi_d = nc.dram_tensor("wi", [D, D], F32, kind="ExternalInput").ap()
    bias_d = nc.dram_tensor("bias", [1, D], F32, kind="ExternalInput").ap()
    cfc_d = nc.dram_tensor("cfc", [D, D], F32, kind="ExternalInput").ap()
    cfs_d = nc.dram_tensor("cfs", [D, D], F32, kind="ExternalInput").ap()
    cs1_d = nc.dram_tensor("cs1", [2, D, 128], F32, kind="ExternalInput").ap()
    maskc_d = nc.dram_tensor("maskc", [128, 128], BF16, kind="ExternalInput").ap()
    idf_d = nc.dram_tensor("idf", [128, 128], F32, kind="ExternalInput").ap()
    ones16_d = nc.dram_tensor("ones16", [1, 128], FP16, kind="ExternalInput").ap()
    id16_d = nc.dram_tensor("id16", [128, 128], FP16, kind="ExternalInput").ap()
    idb_d = nc.dram_tensor("idb", [128, 128], BF16, kind="ExternalInput").ap()
    mask8_d = nc.dram_tensor("mask8", [128, 8], F32, kind="ExternalInput").ap()

    y_d = nc.dram_tensor("y", [B, TPC, D], F32, kind="ExternalOutput").ap()

    scr_xb = nc.dram_tensor("scr_xb", [TPC, D], BF16, kind="Internal")
    scr_k = [nc.dram_tensor(f"scr_k{i}", [D, 128], BF16, kind="Internal") for i in range(3)]
    scr_o = [nc.dram_tensor(f"scr_o{i}", [128, D], BF16, kind="Internal") for i in range(2)]
    scr_f = [nc.dram_tensor(f"scr_f{i}", [128, D], FP16, kind="Internal") for i in range(2)]
    sg_in = nc.dram_tensor("sg_in", [128, 520], F32, kind="Internal")
    sg_out = nc.dram_tensor("sg_out", [8 * 128, 520], F32, kind="Internal")
    mg_in = nc.dram_tensor("mg_in", [128, D], FP16, kind="Internal")
    mg_out = nc.dram_tensor("mg_out", [8 * 128, D], FP16, kind="Internal")

    es = ExitStack()
    dSs = [es.enter_context(nc.semaphore(f"dS{i}")) for i in range(8)]
    dAs = [es.enter_context(nc.semaphore(f"dA{i}")) for i in range(4)]
    dGs = [es.enter_context(nc.semaphore(f"dG{i}")) for i in range(4)]
    ccS = es.enter_context(nc.semaphore("ccS"))
    cP = es.enter_context(nc.semaphore("cP"))
    cV = es.enter_context(nc.semaphore("cV"))
    cA = es.enter_context(nc.semaphore("cA"))
    cG = es.enter_context(nc.semaphore("cG"))
    p = Prog(nc, tuple(dSs + dAs + dGs) + (ccS, cP, cV, cA, cG))
    p.auto_self = {"vector": cV, "scalar": cA, "gpsimd": cG}

    qrot = {"sync": [dSs, 0], "scalar": [dAs, 0], "gpsimd": [dGs, 0]}

    def dma(qe, out, in_, sem_unused=None, waits=(), **kw):
        fam = qrot[qe]
        s = fam[0][fam[1]]
        fam[1] = (fam[1] + 1) % len(fam[0])
        throttle = (s, p.val(s))
        tok = p.emit(qe, lambda e, out=out, in_=in_, kw=kw: e.dma_start(out=out, in_=in_, **kw),
                     waits=list(waits) + [throttle], inc=s, inc_n=16)
        return (s, tok)

    mes = ExitStack()
    w32 = mes.enter_context(nc.sbuf_tensor("w32", [128, 16384], F32))
    cs1 = mes.enter_context(nc.sbuf_tensor("cs1s", [128, 2048], F32))
    cful = mes.enter_context(nc.sbuf_tensor("cful", [128, 16384], F32))
    q1 = mes.enter_context(nc.sbuf_tensor("q1", [128, 1024], F32))
    q2 = mes.enter_context(nc.sbuf_tensor("q2", [128, 1024], F32))
    q1T = mes.enter_context(nc.sbuf_tensor("q1T", [128, 1024], F32))
    q2T = mes.enter_context(nc.sbuf_tensor("q2T", [128, 1024], F32))
    idf = mes.enter_context(nc.sbuf_tensor("idf_s", [128, 128], F32))
    msb = mes.enter_context(nc.sbuf_tensor("msb", [128, 1024], FP16))
    pT = mes.enter_context(nc.psum_tensor("pT", [128, 128], F32))
    mps = ExitStack()
    p13 = mps.enter_context(nc.psum_tensor("p13", [128, 1024], F32))
    p2p = mps.enter_context(nc.psum_tensor("p2p", [128, 1024], F32))
    p4p = mps.enter_context(nc.psum_tensor("p4p", [128, 1024], F32))

    mload_pairs = [
        dma("sync", w32[:, 0:8192], wr_d.rearrange("(dc p) k -> p dc k", p=128)),
        dma("sync", w32[:, 8192:16384], wi_d.rearrange("(dc p) k -> p dc k", p=128)),
        dma("sync", cs1[:], cs1_d.rearrange("c (dc p) j -> p c dc j", p=128)),
        dma("sync", cful[:, 0:8192], cfc_d.rearrange("(dc p) o -> p dc o", p=128)),
        dma("sync", cful[:, 8192:16384], cfs_d.rearrange("(dc p) o -> p dc o", p=128)),
        dma("sync", idf[:], idf_d)]

    def W32(which, dc, lo, hi):
        return w32[:, which * 8192 + dc * 1024 + lo: which * 8192 + dc * 1024 + hi]

    def CS1(which, dc):
        return cs1[:, which * 1024 + dc * 128: which * 1024 + dc * 128 + 128]

    def CFUL(which, dc, lo, hi):
        return cful[:, which * 8192 + dc * 1024 + lo: which * 8192 + dc * 1024 + hi]

    tk = None
    for half in range(2):
        lo, hi = half * 512, half * 512 + 512
        pairs = [(0, 0)] * 0
        n = 0
        for (wch, cch) in ((0, 0), (1, 1)):
            for dc in range(8):
                tk = p.emit("tensor",
                            lambda e, cch=cch, wch=wch, dc=dc, lo=lo, hi=hi, n=n: e.matmul(
                                p13[:, lo:hi], CS1(cch, dc), W32(wch, dc, lo, hi),
                                start=(n == 0), stop=(n == 15)),
                            waits=(mload_pairs if n == 0 else []),
                            inc=cP if n == 15 else None)
                n += 1
        for dc in range(8):
            tk = p.emit("tensor",
                        lambda e, dc=dc, lo=lo, hi=hi: e.matmul(
                            p2p[:, lo:hi], CS1(1, dc), W32(0, dc, lo, hi),
                            start=(dc == 0), stop=(dc == 7)),
                        inc=cP if dc == 7 else None)
        for dc in range(8):
            tk = p.emit("tensor",
                        lambda e, dc=dc, lo=lo, hi=hi: e.matmul(
                            p4p[:, lo:hi], CS1(0, dc), W32(1, dc, lo, hi),
                            start=(dc == 0), stop=(dc == 7)),
                        inc=cP if dc == 7 else None)
    t_stage1 = tk if tk is not None else 0
    t_stage1 = p.val(cP)
    tv = p.emit("vector", lambda e: e.tensor_copy(q1[:], p13[:]), waits=[(cP, t_stage1)], inc=cV)
    tv = p.emit("vector", lambda e: e.tensor_copy(q2T[:], p4p[:]), inc=cV)
    tv = p.emit("vector", lambda e: e.tensor_tensor(q2[:], p2p[:], q2T[:], ALU.subtract), inc=cV)
    t_qev = p.val(cV)
    for src, dst in ((q1, q1T), (q2, q2T)):
        for kc in range(8):
            tt = p.emit("tensor",
                        lambda e, src=src, kc=kc: e.transpose(
                            pT[:], src[:, kc * 128: kc * 128 + 128], idf[:]),
                        waits=[(cV, p.val(cV))], inc=cP)
            p.emit("vector",
                   lambda e, dst=dst, kc=kc: e.tensor_copy(dst[:, kc * 128: kc * 128 + 128], pT[:]),
                   waits=[(cP, tt)], inc=cV)
    t_qT = p.val(cV)
    mps.close()
    pM = mes.enter_context(nc.psum_tensor("pM", [128, 1024], F32))
    n = 0
    for half in range(2):
        lo, hi = half * 512, half * 512 + 512
        n = 0
        for which, qt in ((0, q1T), (1, q2T)):
            for kc in range(8):
                tk = p.emit("tensor",
                            lambda e, which=which, qt=qt, kc=kc, lo=lo, hi=hi, n=n: e.matmul(
                                pM[:, lo:hi], qt[:, kc * 128: kc * 128 + 128],
                                CFUL(which, kc, lo, hi),
                                start=(n == 0), stop=(n == 15)),
                            waits=([(cV, t_qT)] if n == 0 else []),
                            inc=cP if n == 15 else None)
                n += 1
    t_pm = p.val(cP)
    t_msb = p.emit("vector", lambda e: e.tensor_scalar(msb[:], pM[:], 1.0 / D, None, ALU.mult),
                   waits=[(cP, t_pm)], inc=cV)
    t_mg = dma("sync", mg_in.ap(), msb[:], None, waits=[(cV, t_msb)])
    p.emit("gpsimd", lambda e: e.collective_compute(
        "AllGather", ALU.bypass, replica_groups=[list(range(8))],
        ins=[mg_in.ap().opt()], outs=[mg_out.ap().opt()]),
        waits=[t_mg], inc=ccS)
    t_mcc = p.val(ccS)
    mes.close()

    wqb = es.enter_context(nc.sbuf_tensor("wqb", [128, 8192], BF16))
    wkb = es.enter_context(nc.sbuf_tensor("wkb", [128, 8192], BF16))
    wvb = es.enter_context(nc.sbuf_tensor("wvb", [128, 8192], BF16))
    wob = es.enter_context(nc.sbuf_tensor("wob", [128, 8192], BF16))
    msb2 = es.enter_context(nc.sbuf_tensor("msb2", [128, 8192], FP16))
    xTb = es.enter_context(nc.sbuf_tensor("xTb", [128, 8192], BF16))
    kT = es.enter_context(nc.sbuf_tensor("kT", [128, 8192], BF16))
    qT = es.enter_context(nc.sbuf_tensor("qT", [128, 8192], BF16))
    vhat = es.enter_context(nc.sbuf_tensor("vhat", [128, 8448], BF16))
    ktok = [es.enter_context(nc.sbuf_tensor(f"ktok{i}", [128, 1024], BF16)) for i in range(3)]
    bst = es.enter_context(nc.sbuf_tensor("bst", [128, 8 * 520], BF16))
    run = es.enter_context(nc.sbuf_tensor("run", [128, 520], F32))
    gsl = es.enter_context(nc.sbuf_tensor("gsl", [128, 1040], F32))
    pfA = es.enter_context(nc.sbuf_tensor("pfA", [128, 520], F32))
    pfb16 = es.enter_context(nc.sbuf_tensor("pfb16", [128, 520], BF16))
    amask = [es.enter_context(nc.sbuf_tensor(f"amask{i}", [128, 512], BF16)) for i in range(2)]
    o_sb = [es.enter_context(nc.sbuf_tensor(f"o_sb{i}", [128, 1024], BF16)) for i in range(2)]
    oT = es.enter_context(nc.sbuf_tensor("oT", [128, 1024], BF16))
    x_tok = [es.enter_context(nc.sbuf_tensor(f"x_tok{i}", [128, 1024], F32)) for i in range(2)]
    t1 = es.enter_context(nc.sbuf_tensor("t1", [128, 1024], F32))
    sq = es.enter_context(nc.sbuf_tensor("sq", [128, 512], BF16))

    x1h = es.enter_context(nc.sbuf_tensor("x1h", [128, 1024], FP16))
    x1T = [es.enter_context(nc.sbuf_tensor(f"x1T{i}", [128, 1024], FP16)) for i in range(2)]
    out_sb = es.enter_context(nc.sbuf_tensor("out_sb", [128, 1024], F32))
    eS = [es.enter_context(nc.sbuf_tensor(f"eS{i}", [128, 512], BF16)) for i in range(2)]
    rS = [es.enter_context(nc.sbuf_tensor(f"rS{i}", [128, 512], BF16)) for i in range(2)]
    stat = es.enter_context(nc.sbuf_tensor("stat", [128, 16], F32))
    rden = es.enter_context(nc.sbuf_tensor("rden", [128, 8], F32))
    maskc = es.enter_context(nc.sbuf_tensor("maskc_s", [128, 128], BF16))
    ones16 = es.enter_context(nc.sbuf_tensor("ones16_s", [1, 128], FP16))
    id16 = es.enter_context(nc.sbuf_tensor("id16_s", [128, 128], FP16))
    idb = es.enter_context(nc.sbuf_tensor("idb_s", [128, 128], BF16))
    bias16 = es.enter_context(nc.sbuf_tensor("bias16", [1, 1024], FP16))
    mask8 = es.enter_context(nc.sbuf_tensor("mask8_s", [128, 8], F32))

    G = [es.enter_context(nc.psum_tensor(f"G{i}", [128, 512], F32)) for i in range(5)]
    ps_state = [es.enter_context(nc.psum_tensor(f"st{i}", [128, 512], F32)) for i in range(2)]
    ps_z = es.enter_context(nc.psum_tensor("stz", [128, 512], F32))
    G = G + [ps_state[0], ps_state[1], ps_z]

    m_guards = [(cP, t_pm), (cV, t_msb), t_mg]
    wload_pairs = [
        dma("gpsimd", wqb[:], wq_d.rearrange("(kc p) o -> p kc o", p=128), None, waits=m_guards),
        dma("gpsimd", wkb[:], wk_d.rearrange("(kc p) o -> p kc o", p=128), None, waits=m_guards),
        dma("gpsimd", wvb[:], wv_d.rearrange("(kc p) o -> p kc o", p=128), None, waits=m_guards),
        dma("gpsimd", wob[:], wo_d.rearrange("(kc p) o -> p kc o", p=128), None, waits=m_guards)]
    dma("sync", maskc[:], maskc_d, None, waits=m_guards)
    dma("sync", ones16[:], ones16_d, None, waits=m_guards)
    dma("sync", id16[:], id16_d, None, waits=m_guards)
    dma("sync", idb[:], idb_d, None, waits=m_guards)
    t_cload = dma("sync", mask8[:], mask8_d, None, waits=m_guards)
    t_bias = dma("gpsimd", bias16[:], bias_d, None, waits=m_guards)
    t_init = p.emit("vector", lambda e: e.memset(vhat[:], 1.0), inc=cV)
    t_mld = dma("sync", msb2[:], mg_out.ap().rearrange("(kc p) o -> p kc o", p=128), None,
                waits=[(ccS, t_mcc)])

    bank_free = {i: 0 for i in range(8)}

    ROT = [2, 3, 4]

    def bf_wait(i):
        v = bank_free[i]
        if isinstance(v, tuple):
            return ({"cG": cG, "cA": cA, "cV": cV}[v[0]], v[1])
        return (cV, v)
    amask_free = {0: 0, 1: 0}
    prev_ow = {0: None, 1: None}
    prev_pet = {0: 0, 1: 0}
    prev_or = [None]
    prev_fw = {0: None, 1: None}
    prev_wo = [0]
    prev_ffn = [0]
    prev_xt_use = {0: 0, 1: 0}
    prev_ln2_apply = [0]
    rot = {"n": 0}
    fmap_state = {"n": 0, "last": [0, 0]}

    def fmap_evac(psrc, dst_ap, wtok):
        sl_ = fmap_state["n"] % 2
        fmap_state["n"] += 1
        ta = p.emit("scalar", lambda e, psrc=psrc, sl_=sl_: e.activation(eS[sl_][:], psrc[:], AF.Exp),
                    waits=list(wtok) + [(cV, fmap_state["last"][sl_])], inc=cA)
        ta = p.emit("scalar", lambda e, psrc=psrc, sl_=sl_: e.activation(rS[sl_][:], psrc[:], AF.Relu),
                    inc=cA)
        tv = p.emit("vector", lambda e, dst_ap=dst_ap, sl_=sl_: e.scalar_tensor_tensor(
            dst_ap, eS[sl_][:], 1.0, rS[sl_][:], ALU.min, ALU.add),
            waits=[(cA, ta)], inc=cV)
        fmap_state["last"][sl_] = tv
        return tv

    ln_state = {"last_act": 0}

    def ln_from_sum(src_sb, dst_ap, col, dst_waits=(), sum_act=None):
        tv = p.emit("vector", lambda e, col=col: e.tensor_tensor(
            stat[:, col:col + 1], stat[:, col + 6:col + 7], stat[:, col + 7:col + 8], ALU.add),
            waits=([(cA, sum_act)] if sum_act else []), inc=cV)
        tv = p.emit("vector", lambda e, col=col: e.tensor_scalar(
            stat[:, col + 1:col + 2], stat[:, col:col + 1], 1.0 / D, None, ALU.mult), inc=cV)
        tv = p.emit("vector", lambda e, col=col: e.tensor_scalar(
            stat[:, col + 2:col + 3], stat[:, col:col + 1], -1.0 / D, None, ALU.mult), inc=cV)
        ta = p.emit("scalar", lambda e, src_sb=src_sb, col=col: e.activation(
            sq[:], src_sb[:, 0:512], AF.Square, bias=stat[:, col + 2:col + 3],
            accum_out=stat[:, col + 3:col + 4]),
            waits=[(cV, tv)], inc=cA)
        ta = p.emit("scalar", lambda e, src_sb=src_sb, col=col: e.activation(
            sq[:], src_sb[:, 512:1024], AF.Square, bias=stat[:, col + 2:col + 3],
            accum_out=stat[:, col + 4:col + 5]), inc=cA)
        tv = p.emit("vector", lambda e, col=col: e.tensor_tensor(
            stat[:, col + 3:col + 4], stat[:, col + 3:col + 4],
            stat[:, col + 4:col + 5], ALU.add), waits=[(cA, ta)], inc=cV)
        tv = p.emit("vector", lambda e, col=col: e.tensor_scalar(
            stat[:, col + 3:col + 4], stat[:, col + 3:col + 4], 1.0 / D, EPS_LN,
            ALU.mult, ALU.add), inc=cV)
        ta = p.emit("scalar", lambda e, col=col: e.activation(
            stat[:, col + 4:col + 5], stat[:, col + 3:col + 4], AF.Sqrt),
            waits=[(cV, tv)], inc=cA)
        ln_state["last_act"] = ta
        tv = p.emit("vector", lambda e, col=col: e.reciprocal(
            stat[:, col + 5:col + 6], stat[:, col + 4:col + 5]), waits=[(cA, ta)], inc=cV)
        tv = p.emit("vector", lambda e, src_sb=src_sb, dst_ap=dst_ap, col=col: e.tensor_scalar(
            dst_ap, src_sb[:], stat[:, col + 1:col + 2], stat[:, col + 5:col + 6],
            ALU.subtract, ALU.mult), waits=[w for w in dst_waits if w], inc=cV)
        return tv

    NB = int(os.environ.get("KMK_NB", str(B)))

    kw_pairs = {}
    xTp = {}
    kT_half_tok = {}
    vhat_tok = {}
    bst_tok = {}
    bsteff_tok = {}
    prefix_tok = {}
    qproj_tok = {}
    waves_done = {}
    A3_done = {}
    gather_last_stt = [0]
    gather_gld = [None, None]
    prev_state_evac = [0]
    prev_scc = [0]
    prev_sgw = [None]
    prev_kr = {i: None for i in range(3)}
    prev_state_mm = {i: 0 for i in range(3)}
    kr_tok = {}
    state_q_free = [0]

    def emit_xprep(b):
        t_cast = dma("gpsimd", scr_xb.ap(), xs_d[b], None,
                     waits=list(xTp.get(b - 1, [])))
        pairs = []
        for fc in range(8):
            pairs.append(dma("sync", xTb[:, fc * 1024:(fc + 1) * 1024],
                             scr_xb.ap()[:, fc * 128:(fc + 1) * 128], None,
                             waits=[t_cast, (cV, qproj_tok.get((b - 1, 1), 0))] + m_guards,
                             transpose=True))
        xTp[b] = pairs

    def emit_ktproj(b, half, pool=None):
        guards = []
        if b > 0:
            guards.append((cP, A3_done.get((b - 1, half), 0)))
            for cc2 in range(half * 4, half * 4 + 4):
                pr = kw_pairs.get((b - 1, cc2))
                if pr:
                    guards.append(pr)
        for oc in range(8):
            if pool is not None:
                g = pool[oc % len(pool)]
            else:
                g = ROT[rot["n"] % 3]
                rot["n"] += 1
            for kc in range(8):
                tk = p.emit("tensor",
                            lambda e, g=g, kc=kc, oc=oc, half=half: e.matmul(
                                G[g][:],
                                wkb[:, kc * 1024 + oc * 128: kc * 1024 + oc * 128 + 128],
                                xTb[:, kc * 1024 + half * 512: kc * 1024 + half * 512 + 512],
                                start=(kc == 0), stop=(kc == 7)),
                            waits=([bf_wait(g)] + xTp[b] + wload_pairs
                                   if kc == 0 else []),
                            inc=cP if kc == 7 else None)
            tv = fmap_evac(G[g], kT[:, oc * 1024 + half * 512: oc * 1024 + half * 512 + 512],
                           [(cP, tk)] + guards)
            bank_free[g] = tv
        kT_half_tok[(b, half)] = p.val(cV)

    def emit_ktok(b, c):
        sl = c % 3
        t_kw = dma("gpsimd", scr_k[sl].ap().rearrange("(fc p) t -> p fc t", p=128),
                   kT[:].rearrange("p (fc t) -> p fc t", t=1024)[:, :, c * 128:(c + 1) * 128],
                   None, waits=[(cV, kT_half_tok[(b, 0 if c < 4 else 1)]), prev_kr[sl]])
        kw_pairs[(b, c)] = t_kw
        t_kr = dma("sync", ktok[sl][:], scr_k[sl].ap(), None,
                   waits=[t_kw, (cP, prev_state_mm[sl])], transpose=True)
        prev_kr[sl] = t_kr
        kr_tok[(b, c)] = t_kr

    def emit_kchunk(b, c):
        g0 = ROT[rot["n"] % 3]
        rot["n"] += 1
        g1 = ROT[rot["n"] % 3]
        rot["n"] += 1
        for kc in range(8):
            for half2, g in ((0, g0), (1, g1)):
                tk = p.emit("tensor",
                            lambda e, g=g, kc=kc, tt=c, half2=half2: e.matmul(
                                G[g][:],
                                xTb[:, kc * 1024 + tt * 128: kc * 1024 + tt * 128 + 128],
                                wvb[:, kc * 1024 + half2 * 512: kc * 1024 + half2 * 512 + 512],
                                start=(kc == 0), stop=(kc == 7)),
                            waits=([bf_wait(g0), bf_wait(g1), (cV, t_init)] + xTp[b]
                                   if kc == 0 and half2 == 0 else []),
                            inc=cP if kc == 7 and half2 == 1 else None)
        for half2, g in ((0, g0), (1, g1)):
            tv = p.emit("scalar",
                        lambda e, g=g, tt=c, half2=half2: e.activation(
                            vhat[:, tt * 1056 + half2 * 528: tt * 1056 + half2 * 528 + 528]
                            .rearrange("p (h cc) -> p h cc", cc=66)[:, :, 0:64],
                            G[g][:].rearrange("p (h cc) -> p h cc", cc=64), AF.Copy),
                        waits=[(cP, tk), (cP, waves_done.get((b - 1, c), 0))], inc=cA)
            bank_free[g] = ("cA", tv)
        vhat_tok[(b, c)] = ("cA", p.val(cA))
        sl = c % 3
        t_kr = kr_tok[(b, c)]
        if c == 0:
            p.emit("vector", lambda e: e.memset(run[:], 0.0),
                   waits=[prev_sgw[0]], inc=cV)
        for j in range(8):
            ps = ps_state[j // 4]
            tk = p.emit("tensor",
                        lambda e, ps=ps, j=j, sl=sl, c=c: e.matmul(
                            ps[:, (j % 4) * 128:(j % 4) * 128 + 128],
                            ktok[sl][:, j * 128:(j + 1) * 128],
                            vhat[:, c * 1056 + (2 * j) * 66: c * 1056 + (2 * j) * 66 + 132]
                            .rearrange("p (h q) -> p h q", q=66)[:, :, 0:64],
                            start=((j % 4) == 0), stop=((j % 4) == 3)),
                        waits=([t_kr, (cA, vhat_tok[(b, c)][1]), (cV, prev_state_evac[0]),
                                bf_wait(5), bf_wait(6)]
                               if j == 0 else []),
                        inc=None)
        for j in range(8):
            tk = p.emit("tensor",
                        lambda e, j=j, sl=sl, c=c: e.matmul(
                            ps_z[:, j:j + 1],
                            ktok[sl][:, j * 128:(j + 1) * 128],
                            vhat[:, c * 1056 + (2 * j) * 66 + 64: c * 1056 + (2 * j) * 66 + 65],
                            start=(j == 0), stop=(j == 7)),
                        waits=([bf_wait(7)] if j == 0 else []),
                        inc=cP if j == 7 else None)
        prev_state_mm[sl] = tk
        stt_last = [0, 0]
        for bank_i in range(2):
            for parity in range(2):
                stt_last[bank_i] = p.emit("vector", lambda e, bank_i=bank_i, parity=parity:
                       e.scalar_tensor_tensor(
                           run[parity * 64:parity * 64 + 64,
                               bank_i * 260: bank_i * 260 + 260]
                           .rearrange("p (j q) -> p j q", q=65)[:, :, 0:64],
                           ps_state[bank_i][parity * 64:parity * 64 + 64, :]
                           .rearrange("p (j q) -> p j q", q=128)[:, :, parity * 64:parity * 64 + 64],
                           1.0,
                           run[parity * 64:parity * 64 + 64,
                               bank_i * 260: bank_i * 260 + 260]
                           .rearrange("p (j q) -> p j q", q=65)[:, :, 0:64],
                           ALU.mult, ALU.add),
                       waits=[(cP, tk)], inc=cV)
        bank_free[5] = ("cV", stt_last[0])
        bank_free[6] = ("cV", stt_last[1])
        tvz = p.emit("vector", lambda e: e.scalar_tensor_tensor(
            run[:].rearrange("p (j q) -> p j q", q=65)[:, :, 64:65],
            ps_z[:, 0:8].unsqueeze(2), 1.0,
            run[:].rearrange("p (j q) -> p j q", q=65)[:, :, 64:65],
            ALU.mult, ALU.add), inc=cV)
        bank_free[7] = tvz
        tv = p.emit("vector", lambda e, c=c: e.tensor_copy(
            bst[:, c * 520: c * 520 + 520], run[:]),
            waits=[(cP, waves_done.get((b - 1, min(c + 1, NCH - 1)), 0))], inc=cV)
        bst_tok[(b, c)] = tv
        prev_state_evac[0] = tv

    def emit_gather(b):
        t_sgw = dma("sync", sg_in.ap(), run[:], None,
                    waits=[(cV, prev_state_evac[0]), (ccS, prev_scc[0])])
        prev_sgw[0] = t_sgw
        p.emit("gpsimd", lambda e: e.collective_compute(
            "AllGather", ALU.bypass, replica_groups=[list(range(8))],
            ins=[sg_in.ap().opt()], outs=[sg_out.ap().opt()]),
            waits=[t_sgw] + [x for x in gather_gld if x], inc=ccS)
        t_scc = p.val(ccS)
        prev_scc[0] = t_scc
        p.emit("vector", lambda e: e.memset(pfA[:], 0.0),
               waits=[(cV, gather_last_stt[0])], inc=cV)
        stt_tok = [0] * 8
        for r in range(8):
            t_gld = dma("sync", gsl[:, (r % 2) * 520:(r % 2) * 520 + 520],
                        sg_out.ap()[r * 128:(r + 1) * 128, :], None,
                        waits=[(ccS, t_scc), (cV, stt_tok[r - 2] if r >= 2 else 0)])
            gather_gld[r % 2] = t_gld
            stt_tok[r] = p.emit("vector", lambda e, r=r: e.scalar_tensor_tensor(
                pfA[:], gsl[:, (r % 2) * 520:(r % 2) * 520 + 520], mask8[:, r:r + 1], pfA[:],
                ALU.mult, ALU.add),
                waits=[t_gld, t_cload], inc=cV)
        gather_last_stt[0] = stt_tok[7]
        prefix_tok[b] = p.emit("vector", lambda e: e.tensor_copy(pfb16[:], pfA[:]), inc=cV)
        for c2 in range(NCH - 1):
            bsteff_tok[(b, c2)] = p.emit("vector", lambda e, c2=c2: e.tensor_tensor(
                bst[:, c2 * 520: c2 * 520 + 520],
                bst[:, c2 * 520: c2 * 520 + 520], pfb16[:], ALU.add),
                waits=[(cV, bst_tok[(b, c2)])], inc=cV)

    qT_guard = {}
    y_guard = {}
    chunk_info = {}

    def emit_qproj(b, g, pool=None):
        for oc in range(8):
            if pool is not None:
                gb = pool[(oc + 3) % len(pool)]
            else:
                gb = ROT[rot["n"] % 3]
                rot["n"] += 1
            for kc in range(8):
                tk = p.emit("tensor",
                            lambda e, gb=gb, kc=kc, oc=oc, g=g: e.matmul(
                                G[gb][:],
                                wqb[:, kc * 1024 + oc * 128: kc * 1024 + oc * 128 + 128],
                                xTb[:, kc * 1024 + g * 512: kc * 1024 + g * 512 + 512],
                                start=(kc == 0), stop=(kc == 7)),
                            waits=([bf_wait(gb)] if kc == 0 else []),
                            inc=cP if kc == 7 else None)
            tv = fmap_evac(G[gb], qT[:, g * 4096 + oc * 512: g * 4096 + oc * 512 + 512],
                           [(cP, tk), (cP, qT_guard.get((b - 1, g), 0))])
            bank_free[gb] = tv
        qproj_tok[(b, g)] = p.val(cV)

    def emit_waves(b, c):
        g = c // 4
        tg = (c % 4) * 128
        sl = c % 2
        t_xt = dma("sync", x_tok[sl][:], xs_d[b, c * 128:(c + 1) * 128, :], None,
                   waits=[(cV, prev_xt_use[sl])])
        div_tok = 0
        wave_bank = {}
        wave_am = {}

        def emit_A(w):
            half = w // 2
            parity = w % 2
            par = parity * 64
            heads = [half * 8 + parity + 2 * jj for jj in range(4)]
            ga = ROT[rot["n"] % 3]
            rot["n"] += 1
            wave_bank[w] = ga
            for i in range(4):
                h = heads[i]
                fc = h // 2
                tk = p.emit("tensor",
                            lambda e, ga=ga, i=i, fc=fc, par=par, tg=tg, c=c, g=g: e.matmul(
                                G[ga][:, i * 128:(i + 1) * 128],
                                kT[par:par + 64, fc * 1024 + c * 128: fc * 1024 + c * 128 + 128],
                                qT[par:par + 64, g * 4096 + fc * 512 + tg: g * 4096 + fc * 512 + tg + 128],
                                start=True, stop=True,
                                tile_position=(par, 0)),
                            waits=([bf_wait(ga), (cV, qproj_tok[(b, g)])]
                                   if i == 0 else []),
                            inc=cP if i == 3 else None)
            wave_am[w] = p.emit("vector", lambda e, ga=ga, w=w: e.tensor_tensor(
                amask[w % 2][:].rearrange("p (i t) -> p i t", t=128),
                G[ga][:].rearrange("p (i t) -> p i t", t=128),
                maskc[:].unsqueeze(1).broadcast_to([128, 4, 128]),
                ALU.mult),
                waits=[(cP, tk), (cP, amask_free[w % 2])], inc=cV)
            return tk

        def emit_num(w):
            nonlocal div_tok
            half = w // 2
            parity = w % 2
            par = parity * 64
            heads = [half * 8 + parity + 2 * jj for jj in range(4)]
            gnum = wave_bank[w]
            t_am = wave_am[w]
            for i in range(4):
                h = heads[i]
                j = (h % 8) // 2
                scol = (h // 8) * 260 + j * 65
                p.emit("tensor",
                       lambda e, gnum=gnum, i=i, h=h, w=w, c=c: e.matmul(
                           G[gnum][:, i * 65: i * 65 + 65],
                           amask[w % 2][:, i * 128:(i + 1) * 128],
                           vhat[:, c * 1056 + h * 66: c * 1056 + h * 66 + 65],
                           start=(i == 0), stop=False),
                       waits=([(cV, t_am)]))
                state_rhs = (pfb16[par:par + 64, scol: scol + 65] if c == 0 else
                             bst[par:par + 64, (c - 1) * 520 + scol: (c - 1) * 520 + scol + 65])
                tk = p.emit("tensor",
                            lambda e, gnum=gnum, i=i, par=par, tg=tg, h=h, state_rhs=state_rhs, g=g: e.matmul(
                                G[gnum][:, i * 65: i * 65 + 65],
                                qT[par:par + 64, g * 4096 + (h // 2) * 512 + tg: g * 4096 + (h // 2) * 512 + tg + 128],
                                state_rhs,
                                start=False, stop=(i == 3)),
                            waits=[(cV, prefix_tok[b] if c == 0 else bsteff_tok[(b, c - 1)])],
                            inc=cP if i == 3 else None)
            amask_free[w % 2] = tk
            p.emit("vector", lambda e, gnum=gnum: e.tensor_scalar(
                rden[:, 0:4],
                G[gnum][:, 0:260].rearrange("p (i cq) -> p i cq", cq=65)[:, :, 64:65].squeeze(2),
                EPS_ATTN, None, ALU.add),
                waits=[(cP, tk)], inc=cV)
            p.emit("vector", lambda e: e.reciprocal(rden[:, 4:8], rden[:, 0:4]), inc=cV)
            div_tok = p.emit("vector", lambda e, gnum=gnum, half=half, parity=parity, sl=sl: e.tensor_tensor(
                o_sb[sl][:].rearrange("p (hf jj pa cc) -> p hf jj pa cc", hf=2, jj=4, pa=2)[:, half, :, parity, :],
                G[gnum][:, 0:260].rearrange("p (i cq) -> p i cq", cq=65)[:, :, 0:64],
                rden[:, 4:8].unsqueeze(2).broadcast_to([128, 4, 64]),
                ALU.mult),
                waits=[(cP, prev_pet[sl])], inc=cV)
            bank_free[gnum] = div_tok

        emit_A(0)
        emit_A(1)
        last_A = 0
        for w in range(4):
            emit_num(w)
            if w + 2 < 4:
                last_A = emit_A(w + 2)
        waves_done[(b, c)] = p.val(cP)
        if c % 4 == 3:
            A3_done[(b, c // 4)] = p.val(cP)
        chunk_info[(b, c)] = (div_tok, t_xt, sl, None)

    def emit_tail_a(c, b):
        div_tok, t_xt, sl, _unused = chunk_info[(b, c)]
        x1Ts = x1T[c % 2]
        gt = ROT[rot["n"] % 3]
        rot["n"] += 1
        gtv = G[gt][:].bitcast(BF16)
        for fc in range(8):
            tk = p.emit("tensor",
                        lambda e, gt=gt, gtv=gtv, fc=fc, sl=sl: e.transpose(
                            gtv[:, fc * 128:(fc + 1) * 128],
                            o_sb[sl][:, fc * 128:(fc + 1) * 128], idb[:]),
                        waits=([bf_wait(gt), (cV, div_tok), (cP, prev_wo[0])]
                               if fc == 0 else []),
                        inc=cP if fc == 7 else None)
        prev_pet[sl] = tk
        t_oT = p.emit("vector", lambda e, gtv=gtv: e.tensor_copy(oT[:], gtv[:]),
                      waits=[(cP, tk)], inc=cV)
        bank_free[gt] = t_oT
        for fc in range(8):
            for half2 in range(2):
                gb = half2
                tk = p.emit("tensor",
                            lambda e, gb=gb, fc=fc, half2=half2: e.matmul(
                                G[gb][:], oT[:, fc * 128:(fc + 1) * 128],
                                wob[:, fc * 1024 + half2 * 512: fc * 1024 + half2 * 512 + 512],
                                start=(fc == 0), stop=(fc == 7)),
                            waits=([bf_wait(0), bf_wait(1), (cV, t_oT)]
                                   if fc == 0 and half2 == 0 else []),
                            inc=cP if fc == 7 and half2 == 1 else None)
        t_wo = p.val(cP)
        prev_wo[0] = t_wo
        tv = p.emit("vector", lambda e, sl=sl: e.scalar_tensor_tensor(
            t1[:, 0:512], G[0][:], 1.0, x_tok[sl][:, 0:512], ALU.mult, ALU.add,
            accum_out=stat[:, 6:7]),
            waits=[(cP, t_wo), t_xt, (cA, ln_state["last_act"])], inc=cV)
        tv = p.emit("vector", lambda e, sl=sl: e.scalar_tensor_tensor(
            t1[:, 512:1024], G[1][:], 1.0, x_tok[sl][:, 512:1024], ALU.mult, ALU.add,
            accum_out=stat[:, 7:8]), inc=cV)
        bank_free[0] = tv
        bank_free[1] = tv
        prev_xt_use[sl] = tv
        t_x1 = ln_from_sum(t1, x1h[:], 0, dst_waits=[prev_fw[0], prev_fw[1]])
        t_fw = dma("gpsimd", scr_f[sl].ap(), x1h[:], None, waits=[(cV, t_x1)])
        prev_fw[sl] = t_fw
        fr_pairs = []
        for fc in range(8):
            fr_pairs.append(dma("sync", x1Ts[:, fc * 128:(fc + 1) * 128],
                            scr_f[sl].ap()[:, fc * 128:(fc + 1) * 128], None,
                            waits=[t_fw, (cP, prev_ffn[0])], transpose=True))
        chunk_info[(b, c, "a")] = fr_pairs

    def emit_tail_b(c, b):
        div_tok, t_xt, sl, _t_ow = chunk_info[(b, c)]
        fr_pairs = chunk_info[(b, c, "a")]
        x1Ts = x1T[c % 2]
        for fc in range(8):
            for half2 in range(2):
                gb = half2
                p.emit("tensor",
                       lambda e, gb=gb, fc=fc, half2=half2, x1Ts=x1Ts: e.matmul(
                           G[gb][:], x1Ts[:, fc * 128:(fc + 1) * 128],
                           msb2[:, fc * 1024 + half2 * 512: fc * 1024 + half2 * 512 + 512],
                           start=(fc == 0), stop=False),
                       waits=([bf_wait(0), bf_wait(1), t_mld] + fr_pairs
                              if fc == 0 and half2 == 0 else []))
        for half2 in range(2):
            gb = half2
            p.emit("tensor",
                   lambda e, gb=gb, half2=half2: e.matmul(
                       G[gb][:], ones16[:],
                       bias16[0:1, half2 * 512: half2 * 512 + 512],
                       start=False, stop=False),
                   waits=[t_bias])
        for fq in range(4):
            for half2 in range(2):
                gb = half2
                fc = half2 * 4 + fq
                tk = p.emit("tensor",
                            lambda e, gb=gb, fq=fq, fc=fc, x1Ts=x1Ts: e.matmul(
                                G[gb][:, fq * 128:(fq + 1) * 128],
                                x1Ts[:, fc * 128:(fc + 1) * 128], id16[:],
                                start=False, stop=(fq == 3)),
                            inc=cP if fq == 3 and half2 == 1 else None)
        t_ffn = p.val(cP)
        prev_ffn[0] = t_ffn
        ta = p.emit("scalar", lambda e: e.activation(
            out_sb[:, 0:512], G[0][:], AF.Copy, accum_out=stat[:, 14:15]),
            waits=[(cP, t_ffn), (cV, prev_ln2_apply[0]), y_guard.get(0)], inc=cA)
        ta = p.emit("scalar", lambda e: e.activation(
            out_sb[:, 512:1024], G[1][:], AF.Copy, accum_out=stat[:, 15:16]), inc=cA)
        bank_free[0] = ("cA", ta)
        bank_free[1] = ("cA", ta)
        t_out = ln_from_sum(out_sb, out_sb[:], 8, dst_waits=[], sum_act=ta)
        prev_ln2_apply[0] = t_out
        y_guard[0] = dma("gpsimd", y_d[b, c * 128:(c + 1) * 128, :], out_sb[:], None,
                         waits=[(cV, t_out)])

    emit_xprep(0)
    PB5 = [2, 3, 4, 5, 6]
    emit_ktproj(0, 0, pool=PB5)
    for c in range(3):
        emit_ktok(0, c)
    emit_ktproj(0, 1, pool=PB5)
    emit_qproj(0, 0, pool=PB5)
    for c in range(NCH):
        if c == 4:
            emit_qproj(0, 1)
        emit_kchunk(0, c)
        if c + 3 < NCH:
            emit_ktok(0, c + 3)
    emit_gather(0)
    seq = [(b, c) for b in range(NB) for c in range(NCH)]
    for gi, (b, c) in enumerate(seq):
        if c <= 1:
            if gi >= 1:
                emit_tail_a(seq[gi - 1][1], seq[gi - 1][0])
            if gi >= 2:
                emit_tail_b(seq[gi - 2][1], seq[gi - 2][0])
            emit_waves(b, c)
        else:
            emit_waves(b, c)
            if gi >= 1:
                emit_tail_a(seq[gi - 1][1], seq[gi - 1][0])
            if gi >= 2:
                emit_tail_b(seq[gi - 2][1], seq[gi - 2][0])
        if c == 3:
            qT_guard[(b, 0)] = p.val(cP)
        elif c == 7:
            qT_guard[(b, 1)] = p.val(cP)
        if b + 1 < NB:
            if c == 0:
                emit_xprep(b + 1)
            elif c == 4:
                emit_ktproj(b + 1, 0, pool=PB5)
                for c2 in range(3):
                    emit_ktok(b + 1, c2)
                emit_qproj(b + 1, 0, pool=PB5)
            elif c == 5:
                emit_kchunk(b + 1, 0)
                emit_ktok(b + 1, 3)
                emit_kchunk(b + 1, 1)
            elif c == 6:
                emit_kchunk(b + 1, 2)
                emit_kchunk(b + 1, 3)
            elif c == 7:
                emit_ktproj(b + 1, 1)
                for c2 in range(4, 7):
                    emit_ktok(b + 1, c2)
                emit_qproj(b + 1, 1)
                emit_kchunk(b + 1, 4)
                emit_ktok(b + 1, 7)
                for c2 in range(5, NCH):
                    emit_kchunk(b + 1, c2)
                emit_gather(b + 1)
    emit_tail_a(NCH - 1, NB - 1)
    emit_tail_b(NCH - 2, NB - 1)
    emit_tail_b(NCH - 1, NB - 1)

    p.emit("gpsimd", lambda e: e.nop(),
           waits=[(s, p.val(s)) for s in dSs + dAs + dGs])

    block = es.enter_context(nc.Block())
    p.finalize(block)
    es.close()
    return nc


def _host_constants():
    d = np.arange(D, dtype=np.float64)
    ang = 2.0 * np.pi * np.outer(d, d) / D
    C = np.cos(ang)
    Sn = np.sin(ang)
    maskc = np.triu(np.ones((128, 128), np.float32)).astype(BF)
    idf = np.eye(128, dtype=np.float32)
    ones16 = np.ones((1, 128), np.float16)
    id16 = np.eye(128, dtype=np.float16)
    idb = np.eye(128, dtype=np.float32).astype(BF)
    return C, Sn, maskc, idf, ones16, id16, idb


def kernel(L0_attn_x, L0_attn_wq, L0_attn_wk, L0_attn_wv, L0_attn_wo,
           L0_ffn_w_freq_real, L0_ffn_w_freq_imag, L0_ffn_bias):
    x = np.ascontiguousarray(np.asarray(L0_attn_x, dtype=np.float32))
    wq = np.ascontiguousarray(np.asarray(L0_attn_wq, dtype=np.float32))
    wk = np.ascontiguousarray(np.asarray(L0_attn_wk, dtype=np.float32))
    wv = np.ascontiguousarray(np.asarray(L0_attn_wv, dtype=np.float32))
    wo = np.ascontiguousarray(np.asarray(L0_attn_wo, dtype=np.float32))
    wr = np.ascontiguousarray(np.asarray(L0_ffn_w_freq_real, dtype=np.float32))
    wi = np.ascontiguousarray(np.asarray(L0_ffn_w_freq_imag, dtype=np.float32))
    bias = np.ascontiguousarray(np.asarray(L0_ffn_bias, dtype=np.float32)).reshape(1, D)

    C, Sn, maskc, idf, ones16, id16, idb = _host_constants()
    nc = build_program()
    in_maps = []
    for c in range(NCORES):
        rows = slice(c * 128, (c + 1) * 128)
        cs1 = np.stack([np.ascontiguousarray(C[:, rows]).astype(np.float32),
                        np.ascontiguousarray(Sn[:, rows]).astype(np.float32)], 0)
        mask8 = np.zeros((128, 8), np.float32)
        mask8[:, :c] = 1.0
        in_maps.append({
            "xs": np.ascontiguousarray(x[:, c * TPC:(c + 1) * TPC, :]),
            "wq": wq, "wk": wk, "wv": wv, "wo": wo, "wr": wr, "wi": wi,
            "bias": bias,
            "cfc": C.astype(np.float32), "cfs": Sn.astype(np.float32),
            "cs1": cs1, "maskc": maskc, "idf": idf, "ones16": ones16,
            "id16": id16, "idb": idb, "mask8": mask8,
        })
    res = run_bass_kernel_spmd(nc, in_maps, core_ids=list(range(NCORES)))
    out = np.concatenate([np.asarray(res.results[c]["y"]) for c in range(NCORES)], axis=1)
    return out


if __name__ == "__main__":
    nc = build_program()
    print("build ok; instructions:",
          sum(1 for _ in nc.all_instructions()) if hasattr(nc, "all_instructions") else "?")


# version 33
# speedup vs baseline: 1.0285x; 1.0229x over previous
import os
import numpy as np
import ml_dtypes
from contextlib import ExitStack

import concourse.bass as bass
import concourse.mybir as mybir
from concourse.bass_utils import run_bass_kernel_spmd

F32 = mybir.dt.float32
BF16 = mybir.dt.bfloat16
FP16 = mybir.dt.float16
AF = mybir.ActivationFunctionType
ALU = mybir.AluOpType
AX = mybir.AxisListType.X
BF = ml_dtypes.bfloat16

B, S, D = 4, 8192, 1024
NCORES = 8
TPC = S // NCORES
NCH = TPC // 128
NG = TPC // 512
EPS_ATTN = 1e-6
EPS_LN = 1e-5


class Prog:

    ENGINES = ("tensor", "vector", "scalar", "gpsimd", "sync")

    def __init__(self, nc, sems):
        self.nc = nc
        self.sems = sems
        self.streams = {e: [] for e in self.ENGINES}
        self.cnt = {}
        self.last_wait = {e: {} for e in self.ENGINES}
        self.pending = {e: [] for e in self.ENGINES}
        self.auto_self = {}

    def emit(self, engine, fn, waits=(), inc=None, inc_n=1):
        auto = self.auto_self.get(engine)
        waits = list(waits)
        if auto is not None:
            waits.append((auto, self.val(auto)))
            if inc is None:
                inc = auto
        w = []
        for item in list(self.pending[engine]) + list(waits):
            if item is None:
                continue
            sem, val = item
            if val is None or val <= 0:
                continue
            lw = self.last_wait[engine].get(sem.name, 0)
            if val > lw:
                w.append((sem, val))
                self.last_wait[engine][sem.name] = val
        self.pending[engine] = []
        self.streams[engine].append((tuple(w), fn, inc, inc_n))
        if inc is not None:
            self.cnt[inc.name] = self.cnt.get(inc.name, 0) + inc_n
            return self.cnt[inc.name]
        return None

    def val(self, sem):
        return self.cnt.get(sem.name, 0)

    def fence(self):
        snap = [(s, self.val(s)) for s in self.sems]
        for e in self.ENGINES:
            self.pending[e] = list(snap)

    def finalize(self, block):
        for engine in self.ENGINES:
            stream = self.streams[engine]

            def make(engine_stream):
                def body(eng):
                    for waits, fn, inc, inc_n in engine_stream:
                        for semobj, val in waits:
                            eng.wait_ge(semobj, val)
                        instr = fn(eng)
                        if inc is not None:
                            instr.then_inc(inc, inc_n)
                return body
            getattr(block, engine)(make(stream))


def build_program():
    nc = bass.Bass("TRN2", target_bir_lowering=False, debug=False)

    xs_d = nc.dram_tensor("xs", [B, TPC, D], F32, kind="ExternalInput").ap()
    wq_d = nc.dram_tensor("wq", [D, D], F32, kind="ExternalInput").ap()
    wk_d = nc.dram_tensor("wk", [D, D], F32, kind="ExternalInput").ap()
    wv_d = nc.dram_tensor("wv", [D, D], F32, kind="ExternalInput").ap()
    wo_d = nc.dram_tensor("wo", [D, D], F32, kind="ExternalInput").ap()
    wr_d = nc.dram_tensor("wr", [D, D], F32, kind="ExternalInput").ap()
    wi_d = nc.dram_tensor("wi", [D, D], F32, kind="ExternalInput").ap()
    bias_d = nc.dram_tensor("bias", [1, D], F32, kind="ExternalInput").ap()
    cfc_d = nc.dram_tensor("cfc", [D, D], F32, kind="ExternalInput").ap()
    cfs_d = nc.dram_tensor("cfs", [D, D], F32, kind="ExternalInput").ap()
    cs1_d = nc.dram_tensor("cs1", [2, D, 128], F32, kind="ExternalInput").ap()
    maskc_d = nc.dram_tensor("maskc", [128, 128], BF16, kind="ExternalInput").ap()
    idf_d = nc.dram_tensor("idf", [128, 128], F32, kind="ExternalInput").ap()
    ones16_d = nc.dram_tensor("ones16", [1, 128], FP16, kind="ExternalInput").ap()
    id16_d = nc.dram_tensor("id16", [128, 128], FP16, kind="ExternalInput").ap()
    idb_d = nc.dram_tensor("idb", [128, 128], BF16, kind="ExternalInput").ap()
    mask8_d = nc.dram_tensor("mask8", [128, 8], F32, kind="ExternalInput").ap()

    y_d = nc.dram_tensor("y", [B, TPC, D], F32, kind="ExternalOutput").ap()

    scr_xb = nc.dram_tensor("scr_xb", [TPC, D], BF16, kind="Internal")
    scr_k = [nc.dram_tensor(f"scr_k{i}", [D, 128], BF16, kind="Internal") for i in range(3)]
    scr_o = [nc.dram_tensor(f"scr_o{i}", [128, D], BF16, kind="Internal") for i in range(2)]
    scr_f = [nc.dram_tensor(f"scr_f{i}", [128, D], FP16, kind="Internal") for i in range(2)]
    sg_in = nc.dram_tensor("sg_in", [128, 520], F32, kind="Internal")
    sg_out = nc.dram_tensor("sg_out", [8 * 128, 520], F32, kind="Internal")
    mg_in = nc.dram_tensor("mg_in", [128, D], FP16, kind="Internal")
    mg_out = nc.dram_tensor("mg_out", [8 * 128, D], FP16, kind="Internal")

    es = ExitStack()
    dSs = [es.enter_context(nc.semaphore(f"dS{i}")) for i in range(8)]
    dAs = [es.enter_context(nc.semaphore(f"dA{i}")) for i in range(4)]
    dGs = [es.enter_context(nc.semaphore(f"dG{i}")) for i in range(4)]
    ccS = es.enter_context(nc.semaphore("ccS"))
    cP = es.enter_context(nc.semaphore("cP"))
    cV = es.enter_context(nc.semaphore("cV"))
    cA = es.enter_context(nc.semaphore("cA"))
    cG = es.enter_context(nc.semaphore("cG"))
    p = Prog(nc, tuple(dSs + dAs + dGs) + (ccS, cP, cV, cA, cG))
    p.auto_self = {"vector": cV, "scalar": cA, "gpsimd": cG}

    qrot = {"sync": [dSs, 0], "scalar": [dAs, 0], "gpsimd": [dGs, 0]}

    def dma(qe, out, in_, sem_unused=None, waits=(), **kw):
        fam = qrot[qe]
        s = fam[0][fam[1]]
        fam[1] = (fam[1] + 1) % len(fam[0])
        throttle = (s, p.val(s))
        tok = p.emit(qe, lambda e, out=out, in_=in_, kw=kw: e.dma_start(out=out, in_=in_, **kw),
                     waits=list(waits) + [throttle], inc=s, inc_n=16)
        return (s, tok)

    mes = ExitStack()
    w32 = mes.enter_context(nc.sbuf_tensor("w32", [128, 16384], F32))
    cs1 = mes.enter_context(nc.sbuf_tensor("cs1s", [128, 2048], F32))
    cful = mes.enter_context(nc.sbuf_tensor("cful", [128, 16384], F32))
    q1 = mes.enter_context(nc.sbuf_tensor("q1", [128, 1024], F32))
    q2 = mes.enter_context(nc.sbuf_tensor("q2", [128, 1024], F32))
    q1T = mes.enter_context(nc.sbuf_tensor("q1T", [128, 1024], F32))
    q2T = mes.enter_context(nc.sbuf_tensor("q2T", [128, 1024], F32))
    idf = mes.enter_context(nc.sbuf_tensor("idf_s", [128, 128], F32))
    msb = mes.enter_context(nc.sbuf_tensor("msb", [128, 1024], FP16))
    pT = mes.enter_context(nc.psum_tensor("pT", [128, 128], F32))
    mps = ExitStack()
    p13 = mps.enter_context(nc.psum_tensor("p13", [128, 1024], F32))
    p2p = mps.enter_context(nc.psum_tensor("p2p", [128, 1024], F32))
    p4p = mps.enter_context(nc.psum_tensor("p4p", [128, 1024], F32))

    mload_pairs = [
        dma("sync", w32[:, 0:8192], wr_d.rearrange("(dc p) k -> p dc k", p=128)),
        dma("sync", w32[:, 8192:16384], wi_d.rearrange("(dc p) k -> p dc k", p=128)),
        dma("sync", cs1[:], cs1_d.rearrange("c (dc p) j -> p c dc j", p=128)),
        dma("sync", cful[:, 0:8192], cfc_d.rearrange("(dc p) o -> p dc o", p=128)),
        dma("sync", cful[:, 8192:16384], cfs_d.rearrange("(dc p) o -> p dc o", p=128)),
        dma("sync", idf[:], idf_d)]

    def W32(which, dc, lo, hi):
        return w32[:, which * 8192 + dc * 1024 + lo: which * 8192 + dc * 1024 + hi]

    def CS1(which, dc):
        return cs1[:, which * 1024 + dc * 128: which * 1024 + dc * 128 + 128]

    def CFUL(which, dc, lo, hi):
        return cful[:, which * 8192 + dc * 1024 + lo: which * 8192 + dc * 1024 + hi]

    tk = None
    for half in range(2):
        lo, hi = half * 512, half * 512 + 512
        pairs = [(0, 0)] * 0
        n = 0
        for (wch, cch) in ((0, 0), (1, 1)):
            for dc in range(8):
                tk = p.emit("tensor",
                            lambda e, cch=cch, wch=wch, dc=dc, lo=lo, hi=hi, n=n: e.matmul(
                                p13[:, lo:hi], CS1(cch, dc), W32(wch, dc, lo, hi),
                                start=(n == 0), stop=(n == 15)),
                            waits=(mload_pairs if n == 0 else []),
                            inc=cP if n == 15 else None)
                n += 1
        for dc in range(8):
            tk = p.emit("tensor",
                        lambda e, dc=dc, lo=lo, hi=hi: e.matmul(
                            p2p[:, lo:hi], CS1(1, dc), W32(0, dc, lo, hi),
                            start=(dc == 0), stop=(dc == 7)),
                        inc=cP if dc == 7 else None)
        for dc in range(8):
            tk = p.emit("tensor",
                        lambda e, dc=dc, lo=lo, hi=hi: e.matmul(
                            p4p[:, lo:hi], CS1(0, dc), W32(1, dc, lo, hi),
                            start=(dc == 0), stop=(dc == 7)),
                        inc=cP if dc == 7 else None)
    t_stage1 = tk if tk is not None else 0
    t_stage1 = p.val(cP)
    tv = p.emit("vector", lambda e: e.tensor_copy(q1[:], p13[:]), waits=[(cP, t_stage1)], inc=cV)
    tv = p.emit("vector", lambda e: e.tensor_copy(q2T[:], p4p[:]), inc=cV)
    tv = p.emit("vector", lambda e: e.tensor_tensor(q2[:], p2p[:], q2T[:], ALU.subtract), inc=cV)
    t_qev = p.val(cV)
    for src, dst in ((q1, q1T), (q2, q2T)):
        for kc in range(8):
            tt = p.emit("tensor",
                        lambda e, src=src, kc=kc: e.transpose(
                            pT[:], src[:, kc * 128: kc * 128 + 128], idf[:]),
                        waits=[(cV, p.val(cV))], inc=cP)
            p.emit("vector",
                   lambda e, dst=dst, kc=kc: e.tensor_copy(dst[:, kc * 128: kc * 128 + 128], pT[:]),
                   waits=[(cP, tt)], inc=cV)
    t_qT = p.val(cV)
    mps.close()
    pM = mes.enter_context(nc.psum_tensor("pM", [128, 1024], F32))
    n = 0
    for half in range(2):
        lo, hi = half * 512, half * 512 + 512
        n = 0
        for which, qt in ((0, q1T), (1, q2T)):
            for kc in range(8):
                tk = p.emit("tensor",
                            lambda e, which=which, qt=qt, kc=kc, lo=lo, hi=hi, n=n: e.matmul(
                                pM[:, lo:hi], qt[:, kc * 128: kc * 128 + 128],
                                CFUL(which, kc, lo, hi),
                                start=(n == 0), stop=(n == 15)),
                            waits=([(cV, t_qT)] if n == 0 else []),
                            inc=cP if n == 15 else None)
                n += 1
    t_pm = p.val(cP)
    t_msb = p.emit("vector", lambda e: e.tensor_scalar(msb[:], pM[:], 1.0 / D, None, ALU.mult),
                   waits=[(cP, t_pm)], inc=cV)
    t_mg = dma("sync", mg_in.ap(), msb[:], None, waits=[(cV, t_msb)])
    p.emit("gpsimd", lambda e: e.collective_compute(
        "AllGather", ALU.bypass, replica_groups=[list(range(8))],
        ins=[mg_in.ap().opt()], outs=[mg_out.ap().opt()]),
        waits=[t_mg], inc=ccS)
    t_mcc = p.val(ccS)
    mes.close()

    wqb = es.enter_context(nc.sbuf_tensor("wqb", [128, 8192], BF16))
    wkb = es.enter_context(nc.sbuf_tensor("wkb", [128, 8192], BF16))
    wvb = es.enter_context(nc.sbuf_tensor("wvb", [128, 8192], BF16))
    wob = es.enter_context(nc.sbuf_tensor("wob", [128, 8192], BF16))
    msb2 = es.enter_context(nc.sbuf_tensor("msb2", [128, 8192], FP16))
    xTb = es.enter_context(nc.sbuf_tensor("xTb", [128, 8192], BF16))
    kT = es.enter_context(nc.sbuf_tensor("kT", [128, 8192], BF16))
    qT = es.enter_context(nc.sbuf_tensor("qT", [128, 8192], BF16))
    vhat = es.enter_context(nc.sbuf_tensor("vhat", [128, 8448], BF16))
    ktok = [es.enter_context(nc.sbuf_tensor(f"ktok{i}", [128, 1024], BF16)) for i in range(3)]
    bst = es.enter_context(nc.sbuf_tensor("bst", [128, 8 * 520], BF16))
    run = es.enter_context(nc.sbuf_tensor("run", [128, 520], F32))
    gsl = es.enter_context(nc.sbuf_tensor("gsl", [128, 1040], F32))
    pfA = es.enter_context(nc.sbuf_tensor("pfA", [128, 520], F32))
    pfb16 = es.enter_context(nc.sbuf_tensor("pfb16", [128, 520], BF16))
    amask = [es.enter_context(nc.sbuf_tensor(f"amask{i}", [128, 512], BF16)) for i in range(2)]
    o_sb = [es.enter_context(nc.sbuf_tensor(f"o_sb{i}", [128, 1024], BF16)) for i in range(2)]
    oT = es.enter_context(nc.sbuf_tensor("oT", [128, 1024], BF16))
    x_tok = [es.enter_context(nc.sbuf_tensor(f"x_tok{i}", [128, 1024], F32)) for i in range(2)]
    t1 = es.enter_context(nc.sbuf_tensor("t1", [128, 1024], F32))
    sq = es.enter_context(nc.sbuf_tensor("sq", [128, 512], BF16))

    x1h = es.enter_context(nc.sbuf_tensor("x1h", [128, 1024], FP16))
    x1T = [es.enter_context(nc.sbuf_tensor(f"x1T{i}", [128, 1024], FP16)) for i in range(2)]
    out_sb = es.enter_context(nc.sbuf_tensor("out_sb", [128, 1024], F32))
    eS = [es.enter_context(nc.sbuf_tensor(f"eS{i}", [128, 512], BF16)) for i in range(2)]
    rS = [es.enter_context(nc.sbuf_tensor(f"rS{i}", [128, 512], BF16)) for i in range(2)]
    stat = es.enter_context(nc.sbuf_tensor("stat", [128, 16], F32))
    rden = es.enter_context(nc.sbuf_tensor("rden", [128, 8], F32))
    maskc = es.enter_context(nc.sbuf_tensor("maskc_s", [128, 128], BF16))
    ones16 = es.enter_context(nc.sbuf_tensor("ones16_s", [1, 128], FP16))
    id16 = es.enter_context(nc.sbuf_tensor("id16_s", [128, 128], FP16))
    idb = es.enter_context(nc.sbuf_tensor("idb_s", [128, 128], BF16))
    bias16 = es.enter_context(nc.sbuf_tensor("bias16", [1, 1024], FP16))
    mask8 = es.enter_context(nc.sbuf_tensor("mask8_s", [128, 8], F32))

    G = [es.enter_context(nc.psum_tensor(f"G{i}", [128, 512], F32)) for i in range(5)]
    ps_state = [es.enter_context(nc.psum_tensor(f"st{i}", [128, 512], F32)) for i in range(2)]
    ps_z = es.enter_context(nc.psum_tensor("stz", [128, 512], F32))
    G = G + [ps_state[0], ps_state[1], ps_z]

    m_guards = [(cP, t_pm), (cV, t_msb), t_mg]
    wload_pairs = [
        dma("gpsimd", wqb[:], wq_d.rearrange("(kc p) o -> p kc o", p=128), None, waits=m_guards),
        dma("gpsimd", wkb[:], wk_d.rearrange("(kc p) o -> p kc o", p=128), None, waits=m_guards),
        dma("gpsimd", wvb[:], wv_d.rearrange("(kc p) o -> p kc o", p=128), None, waits=m_guards),
        dma("gpsimd", wob[:], wo_d.rearrange("(kc p) o -> p kc o", p=128), None, waits=m_guards)]
    dma("sync", maskc[:], maskc_d, None, waits=m_guards)
    dma("sync", ones16[:], ones16_d, None, waits=m_guards)
    dma("sync", id16[:], id16_d, None, waits=m_guards)
    dma("sync", idb[:], idb_d, None, waits=m_guards)
    t_cload = dma("sync", mask8[:], mask8_d, None, waits=m_guards)
    t_bias = dma("gpsimd", bias16[:], bias_d, None, waits=m_guards)
    t_init = p.emit("vector", lambda e: e.memset(vhat[:], 1.0), inc=cV)
    t_mld = dma("sync", msb2[:], mg_out.ap().rearrange("(kc p) o -> p kc o", p=128), None,
                waits=[(ccS, t_mcc)])

    bank_free = {i: 0 for i in range(8)}

    ROT = [2, 3, 4]

    def bf_wait(i):
        v = bank_free[i]
        if isinstance(v, tuple):
            return ({"cG": cG, "cA": cA, "cV": cV}[v[0]], v[1])
        return (cV, v)
    amask_free = {0: 0, 1: 0}
    prev_ow = {0: None, 1: None}
    prev_pet = {0: 0, 1: 0}
    prev_or = [None]
    prev_fw = {0: None, 1: None}
    prev_wo = [0]
    prev_ffn = [0]
    prev_xt_use = {0: 0, 1: 0}
    prev_ln2_apply = [0]
    rot = {"n": 0}
    fmap_state = {"n": 0, "last": [0, 0]}

    def fmap_evac(psrc, dst_ap, wtok):
        sl_ = fmap_state["n"] % 2
        fmap_state["n"] += 1
        ta = p.emit("scalar", lambda e, psrc=psrc, sl_=sl_: e.activation(eS[sl_][:], psrc[:], AF.Exp),
                    waits=list(wtok) + [(cV, fmap_state["last"][sl_])], inc=cA)
        ta = p.emit("scalar", lambda e, psrc=psrc, sl_=sl_: e.activation(rS[sl_][:], psrc[:], AF.Relu),
                    inc=cA)
        tv = p.emit("vector", lambda e, dst_ap=dst_ap, sl_=sl_: e.scalar_tensor_tensor(
            dst_ap, eS[sl_][:], 1.0, rS[sl_][:], ALU.min, ALU.add),
            waits=[(cA, ta)], inc=cV)
        fmap_state["last"][sl_] = tv
        return tv

    ln_state = {"last_act": 0}

    def ln_from_sum(src_sb, dst_ap, col, dst_waits=(), sum_act=None):
        tv = p.emit("vector", lambda e, col=col: e.tensor_tensor(
            stat[:, col:col + 1], stat[:, col + 6:col + 7], stat[:, col + 7:col + 8], ALU.add),
            waits=([(cA, sum_act)] if sum_act else []), inc=cV)
        tv = p.emit("vector", lambda e, col=col: e.tensor_scalar(
            stat[:, col + 1:col + 2], stat[:, col:col + 1], 1.0 / D, None, ALU.mult), inc=cV)
        tv = p.emit("vector", lambda e, col=col: e.tensor_scalar(
            stat[:, col + 2:col + 3], stat[:, col:col + 1], -1.0 / D, None, ALU.mult), inc=cV)
        ta = p.emit("scalar", lambda e, src_sb=src_sb, col=col: e.activation(
            sq[:], src_sb[:, 0:512], AF.Square, bias=stat[:, col + 2:col + 3],
            accum_out=stat[:, col + 3:col + 4]),
            waits=[(cV, tv)], inc=cA)
        ta = p.emit("scalar", lambda e, src_sb=src_sb, col=col: e.activation(
            sq[:], src_sb[:, 512:1024], AF.Square, bias=stat[:, col + 2:col + 3],
            accum_out=stat[:, col + 4:col + 5]), inc=cA)
        tv = p.emit("vector", lambda e, col=col: e.tensor_tensor(
            stat[:, col + 3:col + 4], stat[:, col + 3:col + 4],
            stat[:, col + 4:col + 5], ALU.add), waits=[(cA, ta)], inc=cV)
        tv = p.emit("vector", lambda e, col=col: e.tensor_scalar(
            stat[:, col + 3:col + 4], stat[:, col + 3:col + 4], 1.0 / D, EPS_LN,
            ALU.mult, ALU.add), inc=cV)
        ta = p.emit("scalar", lambda e, col=col: e.activation(
            stat[:, col + 4:col + 5], stat[:, col + 3:col + 4], AF.Sqrt),
            waits=[(cV, tv)], inc=cA)
        ln_state["last_act"] = ta
        tv = p.emit("vector", lambda e, col=col: e.reciprocal(
            stat[:, col + 5:col + 6], stat[:, col + 4:col + 5]), waits=[(cA, ta)], inc=cV)
        tv = p.emit("vector", lambda e, src_sb=src_sb, dst_ap=dst_ap, col=col: e.tensor_scalar(
            dst_ap, src_sb[:], stat[:, col + 1:col + 2], stat[:, col + 5:col + 6],
            ALU.subtract, ALU.mult), waits=[w for w in dst_waits if w], inc=cV)
        return tv

    NB = int(os.environ.get("KMK_NB", str(B)))

    kw_pairs = {}
    xTp = {}
    kT_half_tok = {}
    vhat_tok = {}
    bst_tok = {}
    bsteff_tok = {}
    prefix_tok = {}
    qproj_tok = {}
    waves_done = {}
    A3_done = {}
    gather_last_stt = [0]
    gather_gld = [None, None]
    prev_state_evac = [0]
    prev_scc = [0]
    prev_sgw = [None]
    prev_kr = {i: None for i in range(3)}
    prev_state_mm = {i: 0 for i in range(3)}
    kr_tok = {}
    state_q_free = [0]

    def emit_xprep(b):
        t_cast = dma("gpsimd", scr_xb.ap(), xs_d[b], None,
                     waits=list(xTp.get(b - 1, [])))
        pairs = []
        for fc in range(8):
            pairs.append(dma("sync", xTb[:, fc * 1024:(fc + 1) * 1024],
                             scr_xb.ap()[:, fc * 128:(fc + 1) * 128], None,
                             waits=[t_cast, (cV, qproj_tok.get((b - 1, 1), 0))] + m_guards,
                             transpose=True))
        xTp[b] = pairs

    def emit_ktproj(b, half):
        guards = []
        if b > 0:
            guards.append((cP, A3_done.get((b - 1, half), 0)))
            for cc2 in range(half * 4, half * 4 + 4):
                pr = kw_pairs.get((b - 1, cc2))
                if pr:
                    guards.append(pr)
        for oc in range(8):
            g = ROT[rot["n"] % 3]
            rot["n"] += 1
            for kc in range(8):
                tk = p.emit("tensor",
                            lambda e, g=g, kc=kc, oc=oc, half=half: e.matmul(
                                G[g][:],
                                wkb[:, kc * 1024 + oc * 128: kc * 1024 + oc * 128 + 128],
                                xTb[:, kc * 1024 + half * 512: kc * 1024 + half * 512 + 512],
                                start=(kc == 0), stop=(kc == 7)),
                            waits=([bf_wait(g)] + xTp[b] + wload_pairs
                                   if kc == 0 else []),
                            inc=cP if kc == 7 else None)
            tv = fmap_evac(G[g], kT[:, oc * 1024 + half * 512: oc * 1024 + half * 512 + 512],
                           [(cP, tk)] + guards)
            bank_free[g] = tv
        kT_half_tok[(b, half)] = p.val(cV)

    def emit_ktok(b, c):
        sl = c % 3
        t_kw = dma("gpsimd", scr_k[sl].ap().rearrange("(fc p) t -> p fc t", p=128),
                   kT[:].rearrange("p (fc t) -> p fc t", t=1024)[:, :, c * 128:(c + 1) * 128],
                   None, waits=[(cV, kT_half_tok[(b, 0 if c < 4 else 1)]), prev_kr[sl]])
        kw_pairs[(b, c)] = t_kw
        t_kr = dma("sync", ktok[sl][:], scr_k[sl].ap(), None,
                   waits=[t_kw, (cP, prev_state_mm[sl])], transpose=True)
        prev_kr[sl] = t_kr
        kr_tok[(b, c)] = t_kr

    def emit_kchunk(b, c):
        g0 = ROT[rot["n"] % 3]
        rot["n"] += 1
        g1 = ROT[rot["n"] % 3]
        rot["n"] += 1
        for kc in range(8):
            for half2, g in ((0, g0), (1, g1)):
                tk = p.emit("tensor",
                            lambda e, g=g, kc=kc, tt=c, half2=half2: e.matmul(
                                G[g][:],
                                xTb[:, kc * 1024 + tt * 128: kc * 1024 + tt * 128 + 128],
                                wvb[:, kc * 1024 + half2 * 512: kc * 1024 + half2 * 512 + 512],
                                start=(kc == 0), stop=(kc == 7)),
                            waits=([bf_wait(g0), bf_wait(g1), (cV, t_init)] + xTp[b]
                                   if kc == 0 and half2 == 0 else []),
                            inc=cP if kc == 7 and half2 == 1 else None)
        for half2, g in ((0, g0), (1, g1)):
            tv = p.emit("scalar",
                        lambda e, g=g, tt=c, half2=half2: e.activation(
                            vhat[:, tt * 1056 + half2 * 528: tt * 1056 + half2 * 528 + 528]
                            .rearrange("p (h cc) -> p h cc", cc=66)[:, :, 0:64],
                            G[g][:].rearrange("p (h cc) -> p h cc", cc=64), AF.Copy),
                        waits=[(cP, tk), (cP, waves_done.get((b - 1, c), 0))], inc=cA)
            bank_free[g] = ("cA", tv)
        vhat_tok[(b, c)] = ("cA", p.val(cA))
        sl = c % 3
        t_kr = kr_tok[(b, c)]
        if c == 0:
            p.emit("vector", lambda e: e.memset(run[:], 0.0),
                   waits=[prev_sgw[0]], inc=cV)
        for j in range(8):
            ps = ps_state[j // 4]
            tk = p.emit("tensor",
                        lambda e, ps=ps, j=j, sl=sl, c=c: e.matmul(
                            ps[:, (j % 4) * 128:(j % 4) * 128 + 128],
                            ktok[sl][:, j * 128:(j + 1) * 128],
                            vhat[:, c * 1056 + (2 * j) * 66: c * 1056 + (2 * j) * 66 + 132]
                            .rearrange("p (h q) -> p h q", q=66)[:, :, 0:64],
                            start=((j % 4) == 0), stop=((j % 4) == 3)),
                        waits=([t_kr, (cA, vhat_tok[(b, c)][1]), (cV, prev_state_evac[0])]
                               if j == 0 else []),
                        inc=None)
        for j in range(8):
            tk = p.emit("tensor",
                        lambda e, j=j, sl=sl, c=c: e.matmul(
                            ps_z[:, j:j + 1],
                            ktok[sl][:, j * 128:(j + 1) * 128],
                            vhat[:, c * 1056 + (2 * j) * 66 + 64: c * 1056 + (2 * j) * 66 + 65],
                            start=(j == 0), stop=(j == 7)),
                        waits=([bf_wait(7)] if j == 0 else []),
                        inc=cP if j == 7 else None)
        prev_state_mm[sl] = tk
        for bank_i in range(2):
            for parity in range(2):
                p.emit("vector", lambda e, bank_i=bank_i, parity=parity:
                       e.scalar_tensor_tensor(
                           run[parity * 64:parity * 64 + 64,
                               bank_i * 260: bank_i * 260 + 260]
                           .rearrange("p (j q) -> p j q", q=65)[:, :, 0:64],
                           ps_state[bank_i][parity * 64:parity * 64 + 64, :]
                           .rearrange("p (j q) -> p j q", q=128)[:, :, parity * 64:parity * 64 + 64],
                           1.0,
                           run[parity * 64:parity * 64 + 64,
                               bank_i * 260: bank_i * 260 + 260]
                           .rearrange("p (j q) -> p j q", q=65)[:, :, 0:64],
                           ALU.mult, ALU.add),
                       waits=[(cP, tk)], inc=cV)
        tvz = p.emit("vector", lambda e: e.scalar_tensor_tensor(
            run[:].rearrange("p (j q) -> p j q", q=65)[:, :, 64:65],
            ps_z[:, 0:8].unsqueeze(2), 1.0,
            run[:].rearrange("p (j q) -> p j q", q=65)[:, :, 64:65],
            ALU.mult, ALU.add), inc=cV)
        bank_free[7] = tvz
        tv = p.emit("vector", lambda e, c=c: e.tensor_copy(
            bst[:, c * 520: c * 520 + 520], run[:]),
            waits=[(cP, waves_done.get((b - 1, min(c + 1, NCH - 1)), 0))], inc=cV)
        bst_tok[(b, c)] = tv
        prev_state_evac[0] = tv

    def emit_gather(b):
        t_sgw = dma("sync", sg_in.ap(), run[:], None,
                    waits=[(cV, prev_state_evac[0]), (ccS, prev_scc[0])])
        prev_sgw[0] = t_sgw
        p.emit("gpsimd", lambda e: e.collective_compute(
            "AllGather", ALU.bypass, replica_groups=[list(range(8))],
            ins=[sg_in.ap().opt()], outs=[sg_out.ap().opt()]),
            waits=[t_sgw] + [x for x in gather_gld if x], inc=ccS)
        t_scc = p.val(ccS)
        prev_scc[0] = t_scc
        p.emit("vector", lambda e: e.memset(pfA[:], 0.0),
               waits=[(cV, gather_last_stt[0])], inc=cV)
        stt_tok = [0] * 8
        for r in range(8):
            t_gld = dma("sync", gsl[:, (r % 2) * 520:(r % 2) * 520 + 520],
                        sg_out.ap()[r * 128:(r + 1) * 128, :], None,
                        waits=[(ccS, t_scc), (cV, stt_tok[r - 2] if r >= 2 else 0)])
            gather_gld[r % 2] = t_gld
            stt_tok[r] = p.emit("vector", lambda e, r=r: e.scalar_tensor_tensor(
                pfA[:], gsl[:, (r % 2) * 520:(r % 2) * 520 + 520], mask8[:, r:r + 1], pfA[:],
                ALU.mult, ALU.add),
                waits=[t_gld, t_cload], inc=cV)
        gather_last_stt[0] = stt_tok[7]
        prefix_tok[b] = p.emit("vector", lambda e: e.tensor_copy(pfb16[:], pfA[:]), inc=cV)
        for c2 in range(NCH - 1):
            bsteff_tok[(b, c2)] = p.emit("vector", lambda e, c2=c2: e.tensor_tensor(
                bst[:, c2 * 520: c2 * 520 + 520],
                bst[:, c2 * 520: c2 * 520 + 520], pfb16[:], ALU.add),
                waits=[(cV, bst_tok[(b, c2)])], inc=cV)

    qT_guard = {}
    y_guard = {}
    chunk_info = {}

    def emit_qproj(b, g):
        for oc in range(8):
            gb = ROT[rot["n"] % 3]
            rot["n"] += 1
            for kc in range(8):
                tk = p.emit("tensor",
                            lambda e, gb=gb, kc=kc, oc=oc, g=g: e.matmul(
                                G[gb][:],
                                wqb[:, kc * 1024 + oc * 128: kc * 1024 + oc * 128 + 128],
                                xTb[:, kc * 1024 + g * 512: kc * 1024 + g * 512 + 512],
                                start=(kc == 0), stop=(kc == 7)),
                            waits=([bf_wait(gb)] if kc == 0 else []),
                            inc=cP if kc == 7 else None)
            tv = fmap_evac(G[gb], qT[:, g * 4096 + oc * 512: g * 4096 + oc * 512 + 512],
                           [(cP, tk), (cP, qT_guard.get((b - 1, g), 0))])
            bank_free[gb] = tv
        qproj_tok[(b, g)] = p.val(cV)

    def emit_waves(b, c):
        g = c // 4
        tg = (c % 4) * 128
        sl = c % 2
        t_xt = dma("sync", x_tok[sl][:], xs_d[b, c * 128:(c + 1) * 128, :], None,
                   waits=[(cV, prev_xt_use[sl])])
        div_tok = 0
        wave_bank = {}
        wave_am = {}

        def emit_A(w):
            half = w // 2
            parity = w % 2
            par = parity * 64
            heads = [half * 8 + parity + 2 * jj for jj in range(4)]
            ga = ROT[rot["n"] % 3]
            rot["n"] += 1
            wave_bank[w] = ga
            for i in range(4):
                h = heads[i]
                fc = h // 2
                tk = p.emit("tensor",
                            lambda e, ga=ga, i=i, fc=fc, par=par, tg=tg, c=c, g=g: e.matmul(
                                G[ga][:, i * 128:(i + 1) * 128],
                                kT[par:par + 64, fc * 1024 + c * 128: fc * 1024 + c * 128 + 128],
                                qT[par:par + 64, g * 4096 + fc * 512 + tg: g * 4096 + fc * 512 + tg + 128],
                                start=True, stop=True,
                                tile_position=(par, 0)),
                            waits=([bf_wait(ga), (cV, qproj_tok[(b, g)])]
                                   if i == 0 else []),
                            inc=cP if i == 3 else None)
            wave_am[w] = p.emit("vector", lambda e, ga=ga, w=w: e.tensor_tensor(
                amask[w % 2][:].rearrange("p (i t) -> p i t", t=128),
                G[ga][:].rearrange("p (i t) -> p i t", t=128),
                maskc[:].unsqueeze(1).broadcast_to([128, 4, 128]),
                ALU.mult),
                waits=[(cP, tk), (cP, amask_free[w % 2])], inc=cV)
            return tk

        def emit_num(w):
            nonlocal div_tok
            half = w // 2
            parity = w % 2
            par = parity * 64
            heads = [half * 8 + parity + 2 * jj for jj in range(4)]
            gnum = wave_bank[w]
            t_am = wave_am[w]
            for i in range(4):
                h = heads[i]
                j = (h % 8) // 2
                scol = (h // 8) * 260 + j * 65
                p.emit("tensor",
                       lambda e, gnum=gnum, i=i, h=h, w=w, c=c: e.matmul(
                           G[gnum][:, i * 65: i * 65 + 65],
                           amask[w % 2][:, i * 128:(i + 1) * 128],
                           vhat[:, c * 1056 + h * 66: c * 1056 + h * 66 + 65],
                           start=(i == 0), stop=False),
                       waits=([(cV, t_am)]))
                state_rhs = (pfb16[par:par + 64, scol: scol + 65] if c == 0 else
                             bst[par:par + 64, (c - 1) * 520 + scol: (c - 1) * 520 + scol + 65])
                tk = p.emit("tensor",
                            lambda e, gnum=gnum, i=i, par=par, tg=tg, h=h, state_rhs=state_rhs, g=g: e.matmul(
                                G[gnum][:, i * 65: i * 65 + 65],
                                qT[par:par + 64, g * 4096 + (h // 2) * 512 + tg: g * 4096 + (h // 2) * 512 + tg + 128],
                                state_rhs,
                                start=False, stop=(i == 3)),
                            waits=[(cV, prefix_tok[b] if c == 0 else bsteff_tok[(b, c - 1)])],
                            inc=cP if i == 3 else None)
            amask_free[w % 2] = tk
            p.emit("vector", lambda e, gnum=gnum: e.tensor_scalar(
                rden[:, 0:4],
                G[gnum][:, 0:260].rearrange("p (i cq) -> p i cq", cq=65)[:, :, 64:65].squeeze(2),
                EPS_ATTN, None, ALU.add),
                waits=[(cP, tk)], inc=cV)
            p.emit("vector", lambda e: e.reciprocal(rden[:, 4:8], rden[:, 0:4]), inc=cV)
            div_tok = p.emit("vector", lambda e, gnum=gnum, half=half, parity=parity, sl=sl: e.tensor_tensor(
                o_sb[sl][:].rearrange("p (hf jj pa cc) -> p hf jj pa cc", hf=2, jj=4, pa=2)[:, half, :, parity, :],
                G[gnum][:, 0:260].rearrange("p (i cq) -> p i cq", cq=65)[:, :, 0:64],
                rden[:, 4:8].unsqueeze(2).broadcast_to([128, 4, 64]),
                ALU.mult),
                waits=[(cP, prev_pet[sl])], inc=cV)
            bank_free[gnum] = div_tok

        emit_A(0)
        emit_A(1)
        last_A = 0
        for w in range(4):
            emit_num(w)
            if w + 2 < 4:
                last_A = emit_A(w + 2)
        waves_done[(b, c)] = p.val(cP)
        if c % 4 == 3:
            A3_done[(b, c // 4)] = p.val(cP)
        chunk_info[(b, c)] = (div_tok, t_xt, sl, None)

    def emit_tail_a(c, b):
        div_tok, t_xt, sl, _unused = chunk_info[(b, c)]
        x1Ts = x1T[c % 2]
        gt = ROT[rot["n"] % 3]
        rot["n"] += 1
        gtv = G[gt][:].bitcast(BF16)
        for fc in range(8):
            tk = p.emit("tensor",
                        lambda e, gt=gt, gtv=gtv, fc=fc, sl=sl: e.transpose(
                            gtv[:, fc * 128:(fc + 1) * 128],
                            o_sb[sl][:, fc * 128:(fc + 1) * 128], idb[:]),
                        waits=([bf_wait(gt), (cV, div_tok), (cP, prev_wo[0])]
                               if fc == 0 else []),
                        inc=cP if fc == 7 else None)
        prev_pet[sl] = tk
        t_oT = p.emit("vector", lambda e, gtv=gtv: e.tensor_copy(oT[:], gtv[:]),
                      waits=[(cP, tk)], inc=cV)
        bank_free[gt] = t_oT
        for fc in range(8):
            for half2 in range(2):
                gb = half2
                tk = p.emit("tensor",
                            lambda e, gb=gb, fc=fc, half2=half2: e.matmul(
                                G[gb][:], oT[:, fc * 128:(fc + 1) * 128],
                                wob[:, fc * 1024 + half2 * 512: fc * 1024 + half2 * 512 + 512],
                                start=(fc == 0), stop=(fc == 7)),
                            waits=([bf_wait(0), bf_wait(1), (cV, t_oT)]
                                   if fc == 0 and half2 == 0 else []),
                            inc=cP if fc == 7 and half2 == 1 else None)
        t_wo = p.val(cP)
        prev_wo[0] = t_wo
        tv = p.emit("vector", lambda e, sl=sl: e.scalar_tensor_tensor(
            t1[:, 0:512], G[0][:], 1.0, x_tok[sl][:, 0:512], ALU.mult, ALU.add,
            accum_out=stat[:, 6:7]),
            waits=[(cP, t_wo), t_xt, (cA, ln_state["last_act"])], inc=cV)
        tv = p.emit("vector", lambda e, sl=sl: e.scalar_tensor_tensor(
            t1[:, 512:1024], G[1][:], 1.0, x_tok[sl][:, 512:1024], ALU.mult, ALU.add,
            accum_out=stat[:, 7:8]), inc=cV)
        bank_free[0] = tv
        bank_free[1] = tv
        prev_xt_use[sl] = tv
        t_x1 = ln_from_sum(t1, x1h[:], 0, dst_waits=[prev_fw[0], prev_fw[1]])
        t_fw = dma("gpsimd", scr_f[sl].ap(), x1h[:], None, waits=[(cV, t_x1)])
        prev_fw[sl] = t_fw
        fr_pairs = []
        for fc in range(8):
            fr_pairs.append(dma("sync", x1Ts[:, fc * 128:(fc + 1) * 128],
                            scr_f[sl].ap()[:, fc * 128:(fc + 1) * 128], None,
                            waits=[t_fw, (cP, prev_ffn[0])], transpose=True))
        chunk_info[(b, c, "a")] = fr_pairs

    def emit_tail_b(c, b):
        div_tok, t_xt, sl, _t_ow = chunk_info[(b, c)]
        fr_pairs = chunk_info[(b, c, "a")]
        x1Ts = x1T[c % 2]
        for fc in range(8):
            for half2 in range(2):
                gb = half2
                p.emit("tensor",
                       lambda e, gb=gb, fc=fc, half2=half2, x1Ts=x1Ts: e.matmul(
                           G[gb][:], x1Ts[:, fc * 128:(fc + 1) * 128],
                           msb2[:, fc * 1024 + half2 * 512: fc * 1024 + half2 * 512 + 512],
                           start=(fc == 0), stop=False),
                       waits=([bf_wait(0), bf_wait(1), t_mld] + fr_pairs
                              if fc == 0 and half2 == 0 else []))
        for half2 in range(2):
            gb = half2
            p.emit("tensor",
                   lambda e, gb=gb, half2=half2: e.matmul(
                       G[gb][:], ones16[:],
                       bias16[0:1, half2 * 512: half2 * 512 + 512],
                       start=False, stop=False),
                   waits=[t_bias])
        for fq in range(4):
            for half2 in range(2):
                gb = half2
                fc = half2 * 4 + fq
                tk = p.emit("tensor",
                            lambda e, gb=gb, fq=fq, fc=fc, x1Ts=x1Ts: e.matmul(
                                G[gb][:, fq * 128:(fq + 1) * 128],
                                x1Ts[:, fc * 128:(fc + 1) * 128], id16[:],
                                start=False, stop=(fq == 3)),
                            inc=cP if fq == 3 and half2 == 1 else None)
        t_ffn = p.val(cP)
        prev_ffn[0] = t_ffn
        ta = p.emit("scalar", lambda e: e.activation(
            out_sb[:, 0:512], G[0][:], AF.Copy, accum_out=stat[:, 14:15]),
            waits=[(cP, t_ffn), (cV, prev_ln2_apply[0]), y_guard.get(0)], inc=cA)
        ta = p.emit("scalar", lambda e: e.activation(
            out_sb[:, 512:1024], G[1][:], AF.Copy, accum_out=stat[:, 15:16]), inc=cA)
        bank_free[0] = ("cA", ta)
        bank_free[1] = ("cA", ta)
        t_out = ln_from_sum(out_sb, out_sb[:], 8, dst_waits=[], sum_act=ta)
        prev_ln2_apply[0] = t_out
        y_guard[0] = dma("gpsimd", y_d[b, c * 128:(c + 1) * 128, :], out_sb[:], None,
                         waits=[(cV, t_out)])

    emit_xprep(0)
    emit_ktproj(0, 0)
    for c in range(3):
        emit_ktok(0, c)
    emit_ktproj(0, 1)
    emit_qproj(0, 0)
    for c in range(NCH):
        emit_kchunk(0, c)
        if c + 3 < NCH:
            emit_ktok(0, c + 3)
    emit_gather(0)
    seq = [(b, c) for b in range(NB) for c in range(NCH)]
    emit_qproj(0, 1)
    for gi, (b, c) in enumerate(seq):
        if c <= 1:
            if gi >= 1:
                emit_tail_a(seq[gi - 1][1], seq[gi - 1][0])
            if gi >= 2:
                emit_tail_b(seq[gi - 2][1], seq[gi - 2][0])
            emit_waves(b, c)
        else:
            emit_waves(b, c)
            if gi >= 1:
                emit_tail_a(seq[gi - 1][1], seq[gi - 1][0])
            if gi >= 2:
                emit_tail_b(seq[gi - 2][1], seq[gi - 2][0])
        if c == 3:
            qT_guard[(b, 0)] = p.val(cP)
        elif c == 7:
            qT_guard[(b, 1)] = p.val(cP)
        if b + 1 < NB:
            if c == 0:
                emit_xprep(b + 1)
            elif c == 4:
                emit_ktproj(b + 1, 0)
                for c2 in range(2):
                    emit_ktok(b + 1, c2)
                emit_qproj(b + 1, 0)
            elif c == 5:
                emit_kchunk(b + 1, 0)
                emit_ktok(b + 1, 2)
                emit_kchunk(b + 1, 1)
                emit_ktok(b + 1, 3)
            elif c == 6:
                emit_kchunk(b + 1, 2)
                emit_kchunk(b + 1, 3)
            elif c == 7:
                emit_ktproj(b + 1, 1)
                for c2 in range(4, 7):
                    emit_ktok(b + 1, c2)
                emit_kchunk(b + 1, 4)
                emit_ktok(b + 1, 7)
                for c2 in range(5, NCH):
                    emit_kchunk(b + 1, c2)
                emit_gather(b + 1)
                emit_qproj(b + 1, 1)
    emit_tail_a(NCH - 1, NB - 1)
    emit_tail_b(NCH - 2, NB - 1)
    emit_tail_b(NCH - 1, NB - 1)

    p.emit("gpsimd", lambda e: e.nop(),
           waits=[(s, p.val(s)) for s in dSs + dAs + dGs])

    block = es.enter_context(nc.Block())
    p.finalize(block)
    es.close()
    return nc


def _host_constants():
    d = np.arange(D, dtype=np.float64)
    ang = 2.0 * np.pi * np.outer(d, d) / D
    C = np.cos(ang)
    Sn = np.sin(ang)
    maskc = np.triu(np.ones((128, 128), np.float32)).astype(BF)
    idf = np.eye(128, dtype=np.float32)
    ones16 = np.ones((1, 128), np.float16)
    id16 = np.eye(128, dtype=np.float16)
    idb = np.eye(128, dtype=np.float32).astype(BF)
    return C, Sn, maskc, idf, ones16, id16, idb


def kernel(L0_attn_x, L0_attn_wq, L0_attn_wk, L0_attn_wv, L0_attn_wo,
           L0_ffn_w_freq_real, L0_ffn_w_freq_imag, L0_ffn_bias):
    x = np.ascontiguousarray(np.asarray(L0_attn_x, dtype=np.float32))
    wq = np.ascontiguousarray(np.asarray(L0_attn_wq, dtype=np.float32))
    wk = np.ascontiguousarray(np.asarray(L0_attn_wk, dtype=np.float32))
    wv = np.ascontiguousarray(np.asarray(L0_attn_wv, dtype=np.float32))
    wo = np.ascontiguousarray(np.asarray(L0_attn_wo, dtype=np.float32))
    wr = np.ascontiguousarray(np.asarray(L0_ffn_w_freq_real, dtype=np.float32))
    wi = np.ascontiguousarray(np.asarray(L0_ffn_w_freq_imag, dtype=np.float32))
    bias = np.ascontiguousarray(np.asarray(L0_ffn_bias, dtype=np.float32)).reshape(1, D)

    C, Sn, maskc, idf, ones16, id16, idb = _host_constants()
    nc = build_program()
    in_maps = []
    for c in range(NCORES):
        rows = slice(c * 128, (c + 1) * 128)
        cs1 = np.stack([np.ascontiguousarray(C[:, rows]).astype(np.float32),
                        np.ascontiguousarray(Sn[:, rows]).astype(np.float32)], 0)
        mask8 = np.zeros((128, 8), np.float32)
        mask8[:, :c] = 1.0
        in_maps.append({
            "xs": np.ascontiguousarray(x[:, c * TPC:(c + 1) * TPC, :]),
            "wq": wq, "wk": wk, "wv": wv, "wo": wo, "wr": wr, "wi": wi,
            "bias": bias,
            "cfc": C.astype(np.float32), "cfs": Sn.astype(np.float32),
            "cs1": cs1, "maskc": maskc, "idf": idf, "ones16": ones16,
            "id16": id16, "idb": idb, "mask8": mask8,
        })
    res = run_bass_kernel_spmd(nc, in_maps, core_ids=list(range(NCORES)))
    out = np.concatenate([np.asarray(res.results[c]["y"]) for c in range(NCORES)], axis=1)
    return out


if __name__ == "__main__":
    nc = build_program()
    print("build ok; instructions:",
          sum(1 for _ in nc.all_instructions()) if hasattr(nc, "all_instructions") else "?")
